# version 03
# baseline (speedup 1.0000x reference)
import numpy as np
import ml_dtypes
from contextlib import ExitStack, contextmanager

import concourse.bass as bass
import concourse.mybir as mybir
from concourse.bass_utils import run_bass_kernel_spmd
from concourse.alu_op_type import AluOpType as ALU

F32 = mybir.dt.float32
BF16 = mybir.dt.bfloat16
FP8 = mybir.dt.float8e4
AF = mybir.ActivationFunctionType
AX = mybir.AxisListType

EMB = 1024
FF = 4096
SEQ = 2048
NB = 2
NE = 8
NF = 32
T = 512
NT = 8
TPB = 4
NCORES = 8
MEAN_W = float(2.0 ** -22)
DEBUG_DUMP = False
DBG_SPECS = [("w1t8", [128, 8 * 4096], "FP8"), ("w2t8", [128, 32 * 1024], "FP8"),
             ("mrs", [1, 4096], "F32"), ("rss", [1, 4096], "F32"),
             ("hm", [128, 16], "F32"), ("Gb", [64, 128], "BF16"),
             ("constPP", [128, 16], "F32"), ("woos", [64, 1024], "BF16"),
             ("hT", [128, 32 * 512], "BF16"), ("xnt0", [128, 8 * 512], "BF16"),
             ("xnt1", [128, 8 * 512], "BF16"), ("attn", [128, 8 * 512], "BF16")]
DBG_TW = [("tpart", [128, 32]), ("tcomb", [128, 2]), ("trowS", [1, 128]),
          ("thrrow", [1, 4]), ("thrcol", [128, 2])]


class Fake:
    def __getattr__(self, n):
        return self

    def __call__(self, *a, **k):
        return self

    def __getitem__(self, i):
        return self

    def __enter__(self):
        return self

    def __exit__(self, *a):
        return False


class K:

    def __init__(self, emit, nc=None, M=None):
        self.emit = emit
        self.nc = nc
        self.cnt = {}
        self.M = M if M is not None else {}
        self.sems = {}
        self.es = ExitStack()

    def sem(self, name):
        self.cnt[name] = 0
        if self.emit:
            self.sems[name] = self.es.enter_context(self.nc.semaphore(name))

    def inc(self, instr, s, key=None, n=1):
        self.cnt[s] += n
        if key is not None:
            if not self.emit:
                self.M[(s,) + tuple(key)] = self.cnt[s]
        if self.emit and instr is not None:
            instr.then_inc(self.sems[s], n)

    def wait(self, eng, s, key):
        if self.emit:
            eng.wait_ge(self.sems[s], self.M[(s,) + tuple(key)])

    def dma(self, eng, semname, key, out, in_):
        cur = self.cnt[semname]
        if self.emit and cur:
            eng.wait_ge(self.sems[semname], cur)
        instr = eng.dma_start(out=out, in_=in_) if self.emit else None
        self.inc(instr, semname, key, 16)

    def sb(self, name, shape, dtype=F32):
        if self.emit:
            return self.es.enter_context(self.nc.sbuf_tensor(name, shape, dtype))
        return Fake()

    def ps(self, name, shape, dtype=F32):
        if self.emit:
            return self.es.enter_context(self.nc.psum_tensor(name, shape, dtype))
        return Fake()

    def scope(self):
        if self.emit:
            return self.nc_scope()
        return Fake()

    @contextmanager
    def nc_scope(self):
        with ExitStack() as inner:
            old = self.es
            self.es = inner
            try:
                yield
            finally:
                self.es = old

    @contextmanager
    def block(self, name):
        if self.emit:
            with self.nc.Block(name) as b:
                yield b
        else:
            yield Fake()

    def body(self, blk, engname, fn):
        if self.emit:
            getattr(blk, engname)(fn)
        else:
            fn(Fake())


def construct(k: "K", io):
    nc = k.nc
    dma_sems = (["dp0", "dp1", "dp2", "da0", "da1", "da2", "dw0", "dw1", "dw2",
                 "dxl0", "dxl1", "dxr0", "dxr1", "do0", "do1", "dow0", "dow1", "dor"]
                + ["dc_" + n for n in ("identd", "onesrd", "onescbd", "onescfd",
                                       "cpoold", "b1d", "b2d", "nwd", "mixd", "cbd",
                                       "wooTd", "wviTd", "wvoTd", "btriud", "bvi2d",
                                       "bvo2d", "boo2d")])
    for s in ("t", "v", "s"):
        k.sem(s)
    for s in dma_sems:
        k.sem(s)

    w1t8 = k.sb("w1t8", [128, NE * FF], FP8)
    w2t8 = k.sb("w2t8", [128, NF * EMB], FP8)
    mrs = k.sb("mrs", [1, NB * SEQ])
    rss = k.sb("rss", [1, NB * SEQ])
    hm = k.sb("hm", [128, 16])
    hmv = k.sb("hmv", [128, 16])
    hmb = k.sb("hmb", [128, 16], BF16)
    Gb = k.sb("Gb", [64, NB * 64], BF16)
    constPP = k.sb("constPP", [128, 16])
    woos = k.sb("woos", [64, EMB], BF16)
    sident = k.sb("sident", [128, 128])
    sonesr = k.sb("sonesr", [1, 128])
    sonescb = k.sb("sonescb", [128, 1], BF16)
    sonescf = k.sb("sonescf", [128, 1])
    scpool = k.sb("scpool", [128, 4])
    sb1 = k.sb("sb1", [128, 32])
    sb2 = k.sb("sb2", [128, 8])
    snw = k.sb("snw", [128, 8])
    smix = k.sb("smix", [128, 2])
    scb = k.sb("scb", [128, 128])
    ncbT = k.sb("ncbT", [64, 256])
    soh = k.sb("soh", [128, 4])

    CONST_DMAS = [
        ("identd", sident), ("onesrd", sonesr), ("onescbd", sonescb),
        ("onescfd", sonescf), ("cpoold", scpool), ("b1d", sb1), ("b2d", sb2),
        ("nwd", snw), ("mixd", smix), ("cbd", scb),
    ]

    with k.scope():
        pbuf = [k.sb(f"pbuf{i}", [128, 2048]) for i in range(3)]
        sabs = [k.sb(f"sabs{i}", [128, 2048]) for i in range(2)]
        ssgn = [k.sb(f"ssgn{i}", [128, 2048], BF16) for i in range(2)]
        smsk = k.sb("smsk", [128, 2048], BF16)
        tpart = k.sb("tpart", [128, 32])
        tcomb = k.sb("tcomb", [128, 2])
        trowS = k.sb("trowS", [1, 128])
        thrrow = k.sb("thrrow", [1, 4])
        thrcol = k.sb("thrcol", [128, 2])
        p_tr = k.ps("p_tr", [1, 128])
        p_tb = k.ps("p_tb", [128, 1])

        def w_chunk_src(w, c):
            if w == 0:
                e, h = c // 2, c % 2
                return io["w1td"][e * 128:(e + 1) * 128, h * 2048:(h + 1) * 2048]
            return io["w2td"][c * 256:(c + 1) * 256, :].rearrange("(j p) c -> p j c", p=128)

        wdst = [w1t8, w2t8]

        with k.block("tw") as blk:
            def sync_body(sync):
                for (dname, sdst) in CONST_DMAS:
                    k.dma(sync, "dc_" + dname, ("c", dname), out=sdst[:], in_=io[dname])
                for w in range(2):
                    for c in range(16):
                        u = w * 16 + c
                        if u >= 3:
                            pw, pc = divmod(u - 3, 16)
                            k.wait(sync, "v", ("twred", pw, pc))
                        dsta = pbuf[u % 3][:] if w == 0 else \
                            pbuf[u % 3][:].rearrange("p (j c) -> p j c", j=2)
                        k.dma(sync, f"dp{u % 3}", ("twa", w, c), out=dsta,
                              in_=w_chunk_src(w, c))
                for w in range(2):
                    for c in range(16):
                        u = w * 16 + c
                        if u >= 3:
                            pw, pc = divmod(u - 3, 16)
                            k.wait(sync, "s", ("twsgn", pw, pc))
                        else:
                            k.wait(sync, "v", ("twred", 1, {0: 14, 1: 15, 2: 13}[u]))
                        dst = pbuf[u % 3][:] if w == 0 else \
                            pbuf[u % 3][:].rearrange("p (j c) -> p j c", j=2)
                        k.dma(sync, f"dp{u % 3}", ("twb", w, c), out=dst,
                              in_=w_chunk_src(w, c))
                if DEBUG_DUMP:
                    k.wait(sync, "v", ("twthrc", 1))
                    twd = {"tpart": tpart, "tcomb": tcomb, "trowS": trowS,
                           "thrrow": thrrow, "thrcol": thrcol}
                    for (nm, _shape) in DBG_TW:
                        k.dma(sync, "do1", ("dbgtw", nm), out=io["dbg_" + nm],
                              in_=twd[nm][:])
                    k.wait(sync, "do1", ("dbgtw", DBG_TW[-1][0]))
                for (dname, _sdst) in CONST_DMAS:
                    k.wait(sync, "dc_" + dname, ("c", dname))
            k.body(blk, "sync", sync_body)

            def tensor_body(tensor):
                k.wait(tensor, "dc_identd", ("c", "identd"))
                k.wait(tensor, "dc_onesrd", ("c", "onesrd"))
                for w in range(2):
                    k.wait(tensor, "v", ("twcomb", w))
                    k.inc(tensor.transpose(p_tr[:], tcomb[:, w:w + 1], sident[:]),
                          "t", ("twtr", w))
                    k.wait(tensor, "v", ("twthr", w))
                    k.inc(tensor.matmul(p_tb[:], lhsT=sonesr[:], rhs=thrrow[0:1, w * 2 + 1:w * 2 + 2],
                                        start=True, stop=True), "t", ("twbc", w))
            k.body(blk, "tensor", tensor_body)

            def vector_body(vector):
                for w in range(2):
                    for c in range(16):
                        k.wait(vector, f"dp{(w * 16 + c) % 3}", ("twa", w, c))
                        k.inc(vector.reduce_sum(tpart[:, w * 16 + c:w * 16 + c + 1],
                                                pbuf[(w * 16 + c) % 3][:], axis=AX.X,
                                                apply_absolute_value=True),
                              "v", ("twred", w, c))
                for w in range(2):
                    k.inc(vector.reduce_sum(tcomb[:, w:w + 1], tpart[:, w * 16:w * 16 + 16],
                                            axis=AX.X), "v", ("twcomb", w))
                    k.wait(vector, "t", ("twtr", w))
                    vector.tensor_copy(trowS[:], p_tr[:])
                    k.inc(vector.reduce_sum(thrrow[0:1, w * 2:w * 2 + 1], trowS[:], axis=AX.X),
                          "v", ("twsum", w))
                    k.wait(vector, "v", ("twsum", w))
                    k.inc(vector.tensor_scalar(thrrow[0:1, w * 2 + 1:w * 2 + 2],
                                                thrrow[0:1, w * 2:w * 2 + 1],
                                                scalar1=MEAN_W, scalar2=1e-8,
                                                op0=ALU.mult, op1=ALU.max), "v", ("twthr", w))
                    k.wait(vector, "t", ("twbc", w))
                    k.inc(vector.tensor_scalar(thrcol[:, w:w + 1], p_tb[:], scalar1=0.5,
                                               scalar2=None, op0=ALU.mult),
                          "v", ("twthrc", w))
                for w in range(2):
                    for c in range(16):
                        k.wait(vector, "s", ("twabs", w, c))
                        vector.tensor_scalar(smsk[:], sabs[(w * 16 + c) % 2][:],
                                             scalar1=thrcol[:, w:w + 1], scalar2=None,
                                             op0=ALU.is_ge)
                        dst = wdst[w][:, c * 2048:(c + 1) * 2048]
                        k.inc(vector.tensor_tensor(dst, smsk[:], ssgn[(w * 16 + c) % 2][:],
                                                   op=ALU.mult),
                              "v", ("twt8", w, c))
            k.body(blk, "vector", vector_body)

            def scalar_body(scalar):
                for w in range(2):
                    for c in range(16):
                        k.wait(scalar, f"dp{(w * 16 + c) % 3}", ("twb", w, c))
                        k.wait(scalar, "v", ("twthrc", w))
                        if w * 16 + c >= 2:
                            pw, pc = divmod(w * 16 + c - 2, 16)
                            k.wait(scalar, "v", ("twt8", pw, pc))
                        k.inc(scalar.activation(sabs[(w * 16 + c) % 2][:],
                                                pbuf[(w * 16 + c) % 3][:], AF.Abs),
                              "s", ("twabs", w, c))
                        k.inc(scalar.sign(ssgn[(w * 16 + c) % 2][:], pbuf[(w * 16 + c) % 3][:]),
                              "s", ("twsgn", w, c))
            k.body(blk, "scalar", scalar_body)

    with k.scope():
        pbufA = [k.sb(f"pbufA{i}", [128, 2048]) for i in range(3)]
        sqbA = [k.sb(f"sqbA{i}", [128, 2048]) for i in range(2)]
        xnA = k.sb("xnA", [128, 2048])
        m_row = k.sb("m_row", [1, 2048])
        msq_row = k.sb("msq_row", [1, 2048])
        srow = k.sb("srow", [1, 2048])

        def xa_src(b, e):
            return io["xTd"][b * EMB + e * 128: b * EMB + (e + 1) * 128, :]

        with k.block("pa") as blk:
            def sync_body(sync):
                for p in range(2):
                    for b in range(NB):
                        for e in range(NE):
                            u = p * 16 + b * 8 + e
                            if u >= 3:
                                pu = u - 3
                                pp, pb, pe = pu // 16, (pu % 16) // 8, pu % 8
                                if pp == 0:
                                    k.wait(sync, "t", ("pamm", pb, pe))
                                else:
                                    k.wait(sync, "v", ("paxn", pb, pe))
                            k.dma(sync, f"da{u % 3}", ("pax", p, b, e),
                                  out=pbufA[u % 3][:], in_=xa_src(b, e))
            k.body(blk, "sync", sync_body)

            with k.scope():
                p_s = [k.ps(f"p_s{s}", [1, 512]) for s in range(4)]
                p_q = [k.ps(f"p_q{s}", [1, 512]) for s in range(4)]

                def tensor_body(tensor):
                    for b in range(NB):
                        if b == 1:
                            k.wait(tensor, "v", ("paev", 0))
                        for e in range(NE):
                            k.wait(tensor, "s", ("pasq", b, e))
                            buf = pbufA[(b * 8 + e) % 3]
                            sq = sqbA[e % 2]
                            for s in range(4):
                                tensor.matmul(p_s[s][:], lhsT=sonescf[:],
                                              rhs=buf[:, s * 512:(s + 1) * 512],
                                              start=(e == 0), stop=(e == 7))
                            for s in range(4):
                                last = (s == 3)
                                mm = tensor.matmul(p_q[s][:], lhsT=sonescf[:],
                                                   rhs=sq[:, s * 512:(s + 1) * 512],
                                                   start=(e == 0), stop=(e == 7))
                                if last:
                                    k.inc(mm, "t", ("pamm", b, e))
                k.body(blk, "tensor", tensor_body)

                def scalar_body(scalar):
                    for b in range(NB):
                        for e in range(NE):
                            k.wait(scalar, f"da{(b * 8 + e) % 3}", ("pax", 0, b, e))
                            if b * 8 + e >= 2:
                                pu = b * 8 + e - 2
                                k.wait(scalar, "t", ("pamm", pu // 8, pu % 8))
                            k.inc(scalar.square(sqbA[e % 2][:], pbufA[(b * 8 + e) % 3][:]),
                                  "s", ("pasq", b, e))
                        k.wait(scalar, "v", ("pavar", b))
                        k.inc(scalar.activation(srow[:], msq_row[:], AF.Sqrt,
                                                bias=scpool[0:1, 1:2]),
                              "s", ("pasqrt", b))
                k.body(blk, "scalar", scalar_body)

                def vector_body(vector):
                    for b in range(NB):
                        k.wait(vector, "t", ("pamm", b, 7))
                        for s in range(4):
                            vector.tensor_scalar(m_row[0:1, s * 512:(s + 1) * 512], p_s[s][:],
                                                 scalar1=1.0 / EMB, scalar2=None, op0=ALU.mult)
                        for s in range(4):
                            ts_ev = vector.tensor_scalar(msq_row[0:1, s * 512:(s + 1) * 512],
                                                         p_q[s][:], scalar1=1.0 / EMB,
                                                         scalar2=None, op0=ALU.mult)
                        k.inc(ts_ev, "v", ("paev", b))
                        vector.tensor_tensor(srow[:], m_row[:], m_row[:], op=ALU.mult)
                        k.inc(vector.tensor_tensor(msq_row[:], msq_row[:], srow[:],
                                                   op=ALU.subtract), "v", ("pavar", b))
                        k.wait(vector, "s", ("pasqrt", b))
                        vector.reciprocal(rss[0:1, b * SEQ:(b + 1) * SEQ], srow[:])
                        k.inc(vector.tensor_tensor(mrs[0:1, b * SEQ:(b + 1) * SEQ], m_row[:],
                                                   rss[0:1, b * SEQ:(b + 1) * SEQ], op=ALU.mult),
                              "v", ("parows", b))
                k.body(blk, "vector", vector_body)

            with k.scope():
                p_amr = k.ps("p_amr", [128, 2048])
                p_ars = k.ps("p_ars", [128, 2048])

                def tensor_body2(tensor):
                    for b in range(NB):
                        k.wait(tensor, "v", ("parows", 1))
                        if b == 1:
                            k.wait(tensor, "v", ("paxn", 0, 7))
                        for s in range(4):
                            tensor.matmul(p_amr[:, s * 512:(s + 1) * 512], lhsT=sonesr[:],
                                          rhs=mrs[0:1, b * SEQ + s * 512: b * SEQ + (s + 1) * 512],
                                          start=True, stop=True)
                        for s in range(4):
                            mm = tensor.matmul(p_ars[:, s * 512:(s + 1) * 512], lhsT=sonesr[:],
                                               rhs=rss[0:1, b * SEQ + s * 512: b * SEQ + (s + 1) * 512],
                                               start=True, stop=True)
                        k.inc(mm, "t", ("pabc", b))
                k.body(blk, "tensor", tensor_body2)

                def vector_body2(vector):
                    for b in range(NB):
                        k.wait(vector, "t", ("pabc", b))
                        for e in range(NE):
                            k.wait(vector, f"da{(16 + b * 8 + e) % 3}", ("pax", 1, b, e))
                            buf = pbufA[(16 + b * 8 + e) % 3]
                            vector.tensor_tensor(sqbA[e % 2][:], buf[:], p_ars[:], op=ALU.mult)
                            vector.tensor_tensor(xnA[:], sqbA[e % 2][:], p_amr[:], op=ALU.subtract)
                            k.inc(vector.reduce_sum(hm[:, e * 2 + b:e * 2 + b + 1], xnA[:],
                                                    axis=AX.X),
                                  "v", ("paxn", b, e))
                    k.wait(vector, "v", ("paxn", 1, 7))
                    k.inc(vector.tensor_scalar(hmv[:], hm[:], scalar1=1.0 / SEQ, scalar2=None,
                                               op0=ALU.mult), "v", ("pahmv",))
                    k.inc(vector.tensor_copy(hmb[:], hmv[:]), "v", ("pahmb",))
                k.body(blk, "vector", vector_body2)

    with k.scope():
        pbufS = [k.sb(f"pbufS{i}", [128, 2048]) for i in range(3)]
        womslice = k.sb("womslice", [2, 512])
        omu = k.sb("omu", [64, 128])
        omg = k.sb("omg", [64, 128])
        omb = k.sb("omb", [64, 128], BF16)
        m2s = k.sb("m2s", [64, 128])
        m2b = k.sb("m2b", [64, 128], BF16)
        m3s = k.sb("m3s", [64, 128])
        gtmp = k.sb("gtmp", [64, 128])
        gtmp2 = k.sb("gtmp2", [64, 128])
        sbtriu = k.sb("sbtriu", [64, 64])
        wooT_t = k.sb("wooT_t", [64, EMB])
        wviT_t = k.sb("wviT_t", [128, 512])
        wvoT_t = k.sb("wvoT_t", [64, EMB])
        bvi2s = k.sb("bvi2s", [2, 64])
        bvo2s = k.sb("bvo2s", [2, EMB])
        boo2s = k.sb("boo2s", [2, EMB])
        vqi = k.sb("vqi", [2, 64])
        vqq = k.sb("vqq", [2, 64])
        vnorm = k.sb("vnorm", [2, 1])
        nvq = k.sb("nvq", [2, 64])
        nvqT = k.sb("nvqT", [64, 2])
        csq = k.sb("csq", [128, 128])
        cnorm = k.sb("cnorm", [128, 2])
        sncb = k.sb("sncb", [128, 128])
        scoresR = k.sb("scoresR", [2, 256])
        mxr = k.sb("mxr", [2, 1])
        mxT = k.sb("mxT", [1, 2])
        scT = k.sb("scT", [128, 4])
        vqcT = k.sb("vqcT", [64, 2])
        vcrow = k.sb("vcrow", [2, EMB])
        c2row = k.sb("c2row", [2, EMB])

        def wtriu_src(c):
            e, h = c // 2, c % 2
            return io["wtriud"][e * 128:(e + 1) * 128, h * 2048:(h + 1) * 2048]

        with k.block("sp") as blk:
            def sync_body(sync):
                for (dname, sdst) in [("wooTd", wooT_t), ("wviTd", wviT_t),
                                      ("wvoTd", wvoT_t), ("btriud", sbtriu),
                                      ("bvi2d", bvi2s), ("bvo2d", bvo2s), ("boo2d", boo2s)]:
                    k.dma(sync, "dc_" + dname, ("spc", dname), out=sdst[:], in_=io[dname])
                for c in range(16):
                    if c >= 3:
                        k.wait(sync, "t", ("spom", c - 3))
                    k.dma(sync, f"dw{c % 3}", ("spw", c), out=pbufS[c % 3][:],
                          in_=wtriu_src(c))
                for s in range(8):
                    k.wait(sync, "v", ("spomev", s))
                    for b in range(NB):
                        k.dma(sync, f"dow{b}", ("spomw", s, b),
                              out=io["omdram"][b:b + 1, s * 512:(s + 1) * 512],
                              in_=womslice[b:b + 1, :])
                k.wait(sync, "dow0", ("spomw", 7, 0))
                k.wait(sync, "dow1", ("spomw", 7, 1))
                for b in range(NB):
                    rdsrc = io["omdram"][b:b + 1, :].rearrange("one (r c) -> (one r) c", r=64)
                    k.dma(sync, "dor", ("spomr", b),
                          out=omu[:, b * 64:(b + 1) * 64], in_=rdsrc)
            k.body(blk, "sync", sync_body)

            with k.scope():
                p_om = [k.ps(f"p_om{s}", [2, 512]) for s in range(8)]

                def tensor_body(tensor):
                    k.wait(tensor, "v", ("pahmv",))
                    for c in range(16):
                        e, h = c // 2, c % 2
                        k.wait(tensor, f"dw{c % 3}", ("spw", c))
                        for s4 in range(4):
                            mm = tensor.matmul(p_om[h * 4 + s4][:],
                                               lhsT=hmv[:, 2 * e:2 * e + 2],
                                               rhs=pbufS[c % 3][:, s4 * 512:(s4 + 1) * 512],
                                               start=(e == 0), stop=(e == 7))
                        k.inc(mm, "t", ("spom", c))
                k.body(blk, "tensor", tensor_body)

                def vector_body(vector):
                    for s in range(8):
                        k.wait(vector, "t", ("spom", 15))
                        if s >= 1:
                            k.wait(vector, "dow0", ("spomw", s - 1, 0))
                            k.wait(vector, "dow1", ("spomw", s - 1, 1))
                        k.inc(vector.tensor_copy(womslice[:], p_om[s][:]), "v", ("spomev", s))
                k.body(blk, "vector", vector_body)

            with k.scope():
                p_a = k.ps("p_a", [128, 512])
                p_b2 = k.ps("p_b2", [128, 512])
                p_c = k.ps("p_c", [128, 512])

                def tensor_body2(tensor):
                    for b in range(NB):
                        k.wait(tensor, "v", ("spomu", b))
                        k.inc(tensor.transpose(p_a[0:64, 0:64], omu[:, b * 64:(b + 1) * 64],
                                               sident[0:64, 0:64]), "t", ("sptr", b))
                        k.wait(tensor, "v", ("spomb", b))
                        k.inc(tensor.matmul(p_b2[0:64, 0:64], lhsT=omg[:, b * 64:(b + 1) * 64],
                                            rhs=omg[:, b * 64:(b + 1) * 64], start=True, stop=True),
                              "t", ("spm2", b))
                        k.wait(tensor, "v", ("spm2e", b))
                        k.inc(tensor.matmul(p_c[0:64, 0:64], lhsT=omg[:, b * 64:(b + 1) * 64],
                                            rhs=m2s[:, b * 64:(b + 1) * 64], start=True, stop=True),
                              "t", ("spm3", b))
                    k.wait(tensor, "v", ("spG", 1))
                    k.wait(tensor, "dc_wviTd", ("spc", "wviTd"))
                    for e in range(NE):
                        mm = tensor.matmul(p_a[0:2, 0:64], lhsT=hmv[:, 2 * e:2 * e + 2],
                                           rhs=wviT_t[:, e * 64:(e + 1) * 64],
                                           start=(e == 0), stop=(e == 7))
                    k.inc(mm, "t", ("spvq",))
                    for c in range(2):
                        k.wait(tensor, "v", ("spncb", c))
                        if c == 1:
                            k.wait(tensor, "v", ("spncbtev", 0))
                        k.inc(tensor.transpose(p_b2[0:64, 0:128], sncb[:, c * 64:(c + 1) * 64],
                                               sident[:]), "t", ("spncbt", c))
                    k.wait(tensor, "v", ("spnvq",))
                    k.inc(tensor.transpose(p_c[0:64, 0:2], nvq[:], sident[0:2, 0:2]),
                          "t", ("spnvqt",))
                    k.wait(tensor, "v", ("spncbtev", 1))
                    k.wait(tensor, "v", ("spnvqte",))
                    k.inc(tensor.matmul(p_a[0:2, 0:256], lhsT=nvqT[:], rhs=ncbT[:],
                                        start=True, stop=True), "t", ("spsc",))
                    k.wait(tensor, "v", ("spmx",))
                    k.inc(tensor.transpose(p_b2[0:1, 0:2], mxr[:], sident[0:2, 0:2]),
                          "t", ("spmxt",))
                    k.wait(tensor, "v", ("spmxte",))
                    k.inc(tensor.matmul(p_c[:, 0:2], lhsT=sonesr[:], rhs=mxT[:],
                                        start=True, stop=True), "t", ("spmb",))
                    for c in range(2):
                        k.inc(tensor.transpose(p_a[:, c * 2:c * 2 + 2],
                                               scoresR[:, c * 128:(c + 1) * 128],
                                               sident[0:2, 0:2]), "t", ("spsct", c))
                    k.wait(tensor, "v", ("spoh", 1))
                    for c in range(2):
                        mm = tensor.matmul(p_b2[0:64, 0:2], lhsT=scb[:, c * 64:(c + 1) * 64],
                                           rhs=soh[:, c * 2:c * 2 + 2],
                                           start=(c == 0), stop=(c == 1))
                    k.inc(mm, "t", ("spvc",))
                    k.wait(tensor, "v", ("spvce",))
                    k.wait(tensor, "dc_wvoTd", ("spc", "wvoTd"))
                    for s in range(2):
                        k.inc(tensor.matmul(p_a[0:2, 0:512], lhsT=vqcT[:],
                                            rhs=wvoT_t[:, s * 512:(s + 1) * 512],
                                            start=True, stop=True), "t", ("spvcr", s))
                        k.wait(tensor, "v", ("spvcre", s))
                    k.wait(tensor, "v", ("spc2",))
                    for e in range(NE):
                        k.inc(tensor.transpose(p_c[:, e * 2:e * 2 + 2],
                                               c2row[:, e * 128:(e + 1) * 128],
                                               sident[0:2, 0:2]), "t", ("spct", e))
                k.body(blk, "tensor", tensor_body2)

                def vector_body2(vector):
                    for b in range(NB):
                        k.wait(vector, "dor", ("spomr", b))
                        k.wait(vector, "dc_btriud", ("spc", "btriud"))
                        k.inc(vector.tensor_tensor(omu[:, b * 64:(b + 1) * 64],
                                                   omu[:, b * 64:(b + 1) * 64], sbtriu[:],
                                                   op=ALU.add), "v", ("spomu", b))
                        k.wait(vector, "t", ("sptr", b))
                        k.inc(vector.tensor_tensor(omg[:, b * 64:(b + 1) * 64],
                                                   omu[:, b * 64:(b + 1) * 64], p_a[0:64, 0:64],
                                                   op=ALU.subtract), "v", ("spomb", b))
                        k.wait(vector, "t", ("spm2", b))
                        k.inc(vector.tensor_copy(m2s[:, b * 64:(b + 1) * 64], p_b2[0:64, 0:64]),
                              "v", ("spm2e", b))
                        k.wait(vector, "v", ("spm2e", b))
                        k.wait(vector, "t", ("spm3", b))
                        k.inc(vector.tensor_copy(m3s[:, b * 64:(b + 1) * 64], p_c[0:64, 0:64]),
                              "v", ("spm3e", b))
                        k.wait(vector, "v", ("spm3e", b))
                        bs = slice(b * 64, (b + 1) * 64)
                        vector.tensor_scalar(gtmp[:, bs], m2s[:, bs], scalar1=-0.5, scalar2=None,
                                             op0=ALU.mult)
                        vector.tensor_tensor(gtmp[:, bs], gtmp[:, bs], omg[:, bs], op=ALU.add)
                        vector.tensor_scalar(gtmp2[:, bs], m3s[:, bs], scalar1=1.0 / 6.0,
                                             scalar2=None, op0=ALU.mult)
                        vector.tensor_tensor(gtmp[:, bs], gtmp[:, bs], gtmp2[:, bs], op=ALU.add)
                        vector.tensor_tensor(gtmp[:, bs], gtmp[:, bs], sident[0:64, 0:64],
                                             op=ALU.add)
                        k.inc(vector.tensor_copy(Gb[:, bs], gtmp[:, bs]), "v", ("spG", b))
                    k.wait(vector, "t", ("spvq",))
                    k.wait(vector, "dc_bvi2d", ("spc", "bvi2d"))
                    k.inc(vector.tensor_tensor(vqi[:], p_a[0:2, 0:64], bvi2s[:], op=ALU.add),
                          "v", ("spvqi",))
                    k.wait(vector, "s", ("spvqsq",))
                    k.inc(vector.reduce_sum(vnorm[:], vqq[:], axis=AX.X), "v", ("spvnr",))
                    k.wait(vector, "s", ("spvqn",))
                    vector.tensor_scalar(vnorm[:], vnorm[:], scalar1=1e-12, scalar2=None,
                                         op0=ALU.max)
                    k.inc(vector.reciprocal(vnorm[:], vnorm[:]), "v", ("spvrec",))
                    k.wait(vector, "v", ("spvrec",))
                    k.inc(vector.tensor_scalar(nvq[:], vqi[:], scalar1=vnorm[:], scalar2=None,
                                               op0=ALU.mult), "v", ("spnvq",))
                    k.wait(vector, "s", ("spcbsq",))
                    for c in range(2):
                        rs_cn = vector.reduce_sum(cnorm[:, c:c + 1], csq[:, c * 64:(c + 1) * 64],
                                                  axis=AX.X)
                    k.inc(rs_cn, "v", ("spcn",))
                    k.wait(vector, "s", ("spcbn",))
                    for c in range(2):
                        vector.tensor_scalar(cnorm[:, c:c + 1], cnorm[:, c:c + 1], scalar1=1e-12,
                                             scalar2=None, op0=ALU.max)
                    k.inc(vector.reciprocal(cnorm[:], cnorm[:]), "v", ("spcrec",))
                    k.wait(vector, "v", ("spcrec",))
                    for c in range(2):
                        k.inc(vector.tensor_scalar(sncb[:, c * 64:(c + 1) * 64],
                                                   scb[:, c * 64:(c + 1) * 64],
                                                   scalar1=cnorm[:, c:c + 1], scalar2=None,
                                                   op0=ALU.mult), "v", ("spncb", c))
                    for c in range(2):
                        k.wait(vector, "t", ("spncbt", c))
                        k.inc(vector.tensor_copy(ncbT[:, c * 128:(c + 1) * 128],
                                                 p_b2[0:64, 0:128]), "v", ("spncbtev", c))
                    k.wait(vector, "t", ("spnvqt",))
                    k.inc(vector.tensor_copy(nvqT[:], p_c[0:64, 0:2]), "v", ("spnvqte",))
                    k.wait(vector, "t", ("spsc",))
                    vector.tensor_copy(scoresR[:], p_a[0:2, 0:256])
                    k.inc(vector.reduce_max(mxr[:], scoresR[:], axis=AX.X), "v", ("spmx",))
                    k.wait(vector, "t", ("spmxt",))
                    k.inc(vector.tensor_copy(mxT[:], p_b2[0:1, 0:2]), "v", ("spmxte",))
                    k.wait(vector, "t", ("spsct", 1))
                    for c in range(2):
                        vector.tensor_copy(scT[:, c * 2:c * 2 + 2], p_a[:, c * 2:c * 2 + 2])
                    k.wait(vector, "t", ("spmb",))
                    for c in range(2):
                        k.inc(vector.tensor_tensor(soh[:, c * 2:c * 2 + 2],
                                                   scT[:, c * 2:c * 2 + 2], p_c[:, 0:2],
                                                   op=ALU.is_ge), "v", ("spoh", c))
                    k.wait(vector, "t", ("spvc",))
                    k.inc(vector.tensor_copy(vqcT[:], p_b2[0:64, 0:2]), "v", ("spvce",))
                    k.wait(vector, "dc_bvo2d", ("spc", "bvo2d"))
                    k.wait(vector, "dc_boo2d", ("spc", "boo2d"))
                    for s in range(2):
                        k.wait(vector, "t", ("spvcr", s))
                        k.inc(vector.tensor_tensor(vcrow[:, s * 512:(s + 1) * 512],
                                                   p_a[0:2, 0:512], bvo2s[:, s * 512:(s + 1) * 512],
                                                   op=ALU.add), "v", ("spvcre", s))
                    vector.tensor_scalar(vcrow[:], vcrow[:], scalar1=smix[0:2, 1:2], scalar2=None,
                                         op0=ALU.mult)
                    vector.tensor_scalar(c2row[:], boo2s[:], scalar1=smix[0:2, 0:1], scalar2=None,
                                         op0=ALU.mult)
                    k.inc(vector.tensor_tensor(c2row[:], c2row[:], vcrow[:], op=ALU.add),
                          "v", ("spc2",))
                    k.wait(vector, "t", ("spct", NE - 1))
                    for e in range(NE):
                        cp_c = vector.tensor_copy(constPP[:, e * 2:e * 2 + 2],
                                                  p_c[:, e * 2:e * 2 + 2])
                    k.inc(cp_c, "v", ("spcpp",))
                    k.wait(vector, "dc_wooTd", ("spc", "wooTd"))
                    k.inc(vector.tensor_scalar(woos[:], wooT_t[:], scalar1=smix[0:64, 0:1],
                                               scalar2=None, op0=ALU.mult), "v", ("spwoos",))
                k.body(blk, "vector", vector_body2)

                def scalar_body2(scalar):
                    k.wait(scalar, "v", ("spvqi",))
                    k.inc(scalar.square(vqq[:], vqi[:]), "s", ("spvqsq",))
                    k.wait(scalar, "dc_cbd", ("c", "cbd"))
                    k.inc(scalar.square(csq[:], scb[:]), "s", ("spcbsq",))
                    k.wait(scalar, "v", ("spvnr",))
                    k.inc(scalar.activation(vnorm[:], vnorm[:], AF.Sqrt,
                                            bias=scpool[0:2, 2:3]), "s", ("spvqn",))
                    k.wait(scalar, "v", ("spcn",))
                    k.inc(scalar.activation(cnorm[:], cnorm[:], AF.Sqrt,
                                            bias=scpool[:, 2:3]), "s", ("spcbn",))
                k.body(blk, "scalar", scalar_body2)

    with k.scope():
        hT = k.sb("hT", [128, NF * T], BF16)
        xnt = [k.sb(f"xnt{i}", [128, NE * T], BF16) for i in range(2)]
        attn = k.sb("attn", [128, NE * T], BF16)
        xr_sb = k.sb("xr_sb", [64, T], BF16)
        abuf = [k.sb(f"abuf{i}", [128, T]) for i in range(2)]
        ubuf = [k.sb(f"ubuf{i}", [128, T]) for i in range(2)]
        sqb = [k.sb(f"sqb{i}", [128, T], BF16) for i in range(NE)]
        gtb = [k.sb(f"gtb{i}", [128, T], BF16) for i in range(2)]
        outb = [k.sb(f"outb{i}", [128, T]) for i in range(2)]
        xlnb = [k.sb(f"xlnb{i}", [128, T]) for i in range(2)]
        xresb = [k.sb(f"xresb{i}", [128, T]) for i in range(2)]
        rms_row = k.sb("rms_row", [1, T])
        rstd_row = k.sb("rstd_row", [1, T])

        p_mm = [k.ps(f"p_mm{i}", [128, T]) for i in range(2)]
        p_mr = k.ps("p_mr", [128, T])
        p_rs = k.ps("p_rs", [128, T])
        p_rstd = k.ps("p_rstd", [128, T])
        p_lie = [k.ps(f"p_lie{i}", [128, T]) for i in range(2)]
        p_ms = k.ps("p_ms", [1, T])

        def x_src(t, e):
            b, i = t // TPB, t % TPB
            return io["xTd"][b * EMB + e * 128: b * EMB + (e + 1) * 128,
                             i * T:(i + 1) * T]

        def out_dst(t, e):
            b, i = t // TPB, t % TPB
            return io["outTd"][b * EMB + e * 128: b * EMB + (e + 1) * 128,
                               i * T:(i + 1) * T]

        with k.block("pb") as blk:
            def sync_body(sync):
                for e in range(NE):
                    if e >= 2:
                        k.wait(sync, "v", ("xn", 0, e - 2))
                    k.dma(sync, f"dxl{e % 2}", ("xln", 0, e),
                          out=xlnb[e % 2][:], in_=x_src(0, e))
                for t in range(NT):
                    if t + 1 < NT:
                        for e in range(NE):
                            pe, pt = e - 2, t + 1
                            if pe < 0:
                                pe += 8
                                pt -= 1
                            k.wait(sync, "v", ("xn", pt, pe))
                            k.dma(sync, f"dxl{e % 2}", ("xln", t + 1, e),
                                  out=xlnb[e % 2][:], in_=x_src(t + 1, e))
                    for e in range(NE):
                        if (t, e) != (0, 0) and (t, e) != (0, 1):
                            pe = e - 2
                            pt = t
                            if pe < 0:
                                pe += 8
                                pt -= 1
                            k.wait(sync, "v", ("sg", pt, pe))
                        k.dma(sync, f"dxr{e % 2}", ("xres", t, e),
                              out=xresb[e % 2][:], in_=x_src(t, e))
                        if e >= 2:
                            k.wait(sync, "v", ("sg", t, e - 2))
                            k.dma(sync, f"do{e % 2}", ("od", t, e - 2),
                                  out=out_dst(t, e - 2), in_=outb[e % 2][:])
                    for e in (6, 7):
                        k.wait(sync, "v", ("sg", t, e))
                        k.dma(sync, f"do{e % 2}", ("od", t, e),
                              out=out_dst(t, e), in_=outb[e % 2][:])
                k.wait(sync, "do0", ("od", NT - 1, NE - 2))
                k.wait(sync, "do1", ("od", NT - 1, NE - 1))
                if DEBUG_DUMP:
                    dbg_tensors = {"w1t8": w1t8, "w2t8": w2t8, "mrs": mrs, "rss": rss,
                                   "hm": hm, "Gb": Gb, "constPP": constPP, "woos": woos,
                                   "hT": hT, "xnt0": xnt[0], "xnt1": xnt[1], "attn": attn}
                    for (nm, _shape, _dt) in DBG_SPECS:
                        k.dma(sync, "do0", ("dbg", nm), out=io["dbg_" + nm],
                              in_=dbg_tensors[nm][:])
                    k.wait(sync, "do0", ("dbg", DBG_SPECS[-1][0]))
            k.body(blk, "sync", sync_body)

            def tensor_body(tensor):
                b0 = 0
                for s2, (row, dstp) in enumerate([(mrs, p_mr), (rss, p_rs)]):
                    mm = tensor.matmul(dstp[:], lhsT=sonesr[:],
                                       rhs=row[0:1, 0:T], start=True, stop=True)
                k.inc(mm, "t", ("bc", 0))
                for t in range(NT):
                    b, i = t // TPB, t % TPB
                    if t + 1 < NT:
                        k.wait(tensor, "v", ("xn", t, 7))
                        nb_, ni = (t + 1) // TPB, (t + 1) % TPB
                        c0 = nb_ * SEQ + ni * T
                        tensor.matmul(p_mr[:], lhsT=sonesr[:], rhs=mrs[0:1, c0:c0 + T],
                                      start=True, stop=True)
                        mm = tensor.matmul(p_rs[:], lhsT=sonesr[:], rhs=rss[0:1, c0:c0 + T],
                                           start=True, stop=True)
                        k.inc(mm, "t", ("bc", t + 1))
                    for f in range(NF):
                        if f == 0:
                            k.wait(tensor, "v", ("xn", t, 7))
                            if t > 0:
                                k.wait(tensor, "s", ("silu", t - 1, 6))
                        elif f == 1:
                            if t > 0:
                                k.wait(tensor, "s", ("silu", t - 1, 7))
                        else:
                            k.wait(tensor, "s", ("gelu", t, f - 2))
                        for e in range(NE):
                            mm = tensor.matmul(p_mm[f % 2][:],
                                               lhsT=w1t8[:, e * FF + f * 128: e * FF + (f + 1) * 128],
                                               rhs=xnt[t % 2][:, e * T:(e + 1) * T],
                                               start=(e == 0), stop=(e == NE - 1))
                        k.inc(mm, "t", ("mm1", t, f))
                    if t > 0:
                        k.wait(tensor, "v", ("a", t - 1, 6))
                    k.inc(tensor.matmul(p_lie[0][0:64, :], lhsT=Gb[:, b * 64:(b + 1) * 64],
                                        rhs=xnt[t % 2][0:64, 0:T], start=True, stop=True),
                          "t", ("xr", t))
                    for e in range(NE):
                        if e == 0:
                            k.wait(tensor, "s", ("xrev", t))
                        elif e == 1:
                            if t > 0:
                                k.wait(tensor, "v", ("a", t - 1, 7))
                        else:
                            k.wait(tensor, "v", ("a", t, e - 2))
                        k.inc(tensor.matmul(p_lie[e % 2][:], lhsT=woos[:, e * 128:(e + 1) * 128],
                                            rhs=xr_sb[:], start=True, stop=True),
                              "t", ("lie", t, e))
                    def mm2_group(e2):
                        if e2 == 0:
                            k.wait(tensor, "s", ("gelu", t, NF - 1))
                        elif e2 >= 2:
                            k.wait(tensor, "s", ("silu", t, e2 - 2))
                        for f in range(NF):
                            mm = tensor.matmul(p_mm[e2 % 2][:],
                                               lhsT=w2t8[:, f * EMB + e2 * 128: f * EMB + (e2 + 1) * 128],
                                               rhs=hT[:, f * T:(f + 1) * T],
                                               start=(f == 0), stop=(f == NF - 1))
                        k.inc(mm, "t", ("mm2", t, e2))
                    mm2_group(0)
                    for e in range(NE):
                        k.wait(tensor, "s", ("sq", t, e))
                        if e == 0 and t > 0:
                            k.wait(tensor, "s", ("sqrt", t - 1))
                        k.inc(tensor.matmul(p_ms[:], lhsT=sonescb[:], rhs=sqb[e][:],
                                            start=(e == 0), stop=(e == NE - 1)),
                              "t", ("ms", t, e))
                    k.wait(tensor, "v", ("recip", t))
                    if t > 0:
                        k.wait(tensor, "v", ("sg", t - 1, 7))
                    k.inc(tensor.matmul(p_rstd[:], lhsT=sonesr[:], rhs=rstd_row[:],
                                        start=True, stop=True), "t", ("rstdbc", t))
                    for e2 in range(1, NE):
                        mm2_group(e2)
            k.body(blk, "tensor", tensor_body)

            def vector_body(vector):
                for e in range(NE):
                    k.wait(vector, f"dxl{e % 2}", ("xln", 0, e))
                    k.wait(vector, "t", ("bc", 0))
                    vector.tensor_tensor(ubuf[e % 2][:], xlnb[e % 2][:], p_rs[:], op=ALU.mult)
                    k.inc(vector.tensor_tensor(xnt[0][:, e * T:(e + 1) * T], ubuf[e % 2][:],
                                               p_mr[:], op=ALU.subtract), "v", ("xn", 0, e))
                for t in range(NT):
                    b = t // TPB
                    if t + 1 < NT:
                        for e in range(NE):
                            k.wait(vector, f"dxl{e % 2}", ("xln", t + 1, e))
                            k.wait(vector, "t", ("bc", t + 1))
                            if t >= 1:
                                k.wait(vector, "t", ("xr", t - 1))
                            vector.tensor_tensor(ubuf[e % 2][:], xlnb[e % 2][:], p_rs[:],
                                                 op=ALU.mult)
                            k.inc(vector.tensor_tensor(xnt[(t + 1) % 2][:, e * T:(e + 1) * T],
                                                       ubuf[e % 2][:], p_mr[:], op=ALU.subtract),
                                  "v", ("xn", t + 1, e))
                    for e in range(NE):
                        k.wait(vector, "t", ("lie", t, e))
                        if e >= 2:
                            k.wait(vector, "s", ("sq", t, e - 2))
                        k.inc(vector.scalar_tensor_tensor(
                            out=abuf[e % 2][:], in0=p_lie[e % 2][:],
                            scalar=constPP[:, e * 2 + b:e * 2 + b + 1],
                            in1=xnt[t % 2][:, e * T:(e + 1) * T],
                            op0=ALU.add, op1=ALU.add), "v", ("a", t, e))
                        vector.tensor_scalar(attn[:, e * T:(e + 1) * T], abuf[e % 2][:],
                                             scalar1=snw[:, e:e + 1], scalar2=None, op0=ALU.mult)
                    k.wait(vector, "s", ("sqrt", t))
                    if t >= 1:
                        k.wait(vector, "t", ("rstdbc", t - 1))
                    k.inc(vector.reciprocal(rstd_row[:], rms_row[:]), "v", ("recip", t))
                    for e2 in range(NE):
                        k.wait(vector, "s", ("silu", t, e2))
                        k.wait(vector, "t", ("rstdbc", t))
                        k.wait(vector, f"dxr{e2 % 2}", ("xres", t, e2))
                        if (t, e2) > (0, 1):
                            pe = e2 - 2
                            pt = t
                            if pe < 0:
                                pe += 8
                                pt -= 1
                            k.wait(vector, f"do{pe % 2}", ("od", pt, pe))
                        vector.tensor_tensor(outb[e2 % 2][:], gtb[e2 % 2][:], p_rstd[:],
                                             op=ALU.mult)
                        vector.tensor_tensor(outb[e2 % 2][:], outb[e2 % 2][:],
                                             attn[:, e2 * T:(e2 + 1) * T], op=ALU.mult)
                        k.inc(vector.tensor_tensor(outb[e2 % 2][:], outb[e2 % 2][:],
                                                   xresb[e2 % 2][:], op=ALU.add),
                              "v", ("sg", t, e2))
            k.body(blk, "vector", vector_body)

            def scalar_body(scalar):
                for t in range(NT):
                    for f in range(NF):
                        k.wait(scalar, "t", ("mm1", t, f))
                        k.inc(scalar.activation(hT[:, f * T:(f + 1) * T], p_mm[f % 2][:],
                                                AF.Gelu, bias=sb1[:, f:f + 1]),
                              "s", ("gelu", t, f))
                    k.wait(scalar, "t", ("xr", t))
                    if t > 0:
                        k.wait(scalar, "t", ("lie", t - 1, 7))
                    k.inc(scalar.copy(xr_sb[:], p_lie[0][0:64, :]), "s", ("xrev", t))
                    for e in range(NE):
                        k.wait(scalar, "v", ("a", t, e))
                        if t > 0:
                            k.wait(scalar, "t", ("ms", t - 1, e))
                        k.inc(scalar.square(sqb[e][:], abuf[e % 2][:]), "s", ("sq", t, e))
                    k.wait(scalar, "t", ("ms", t, NE - 1))
                    if t > 0:
                        k.wait(scalar, "v", ("recip", t - 1))
                    k.inc(scalar.activation(rms_row[:], p_ms[:], AF.Sqrt,
                                            bias=scpool[0:1, 0:1], scale=1.0 / EMB),
                          "s", ("sqrt", t))
                    for e2 in range(NE):
                        k.wait(scalar, "t", ("mm2", t, e2))
                        if (t, e2) > (0, 1):
                            pe = e2 - 2
                            pt = t
                            if pe < 0:
                                pe += 8
                                pt -= 1
                            k.wait(scalar, "v", ("sg", pt, pe))
                        k.inc(scalar.activation(gtb[e2 % 2][:], p_mm[e2 % 2][:], AF.Silu,
                                                bias=sb2[:, e2:e2 + 1]),
                              "s", ("silu", t, e2))
            k.body(blk, "scalar", scalar_body)


def build_program():
    nc = bass.Bass("TRN2", target_bir_lowering=False, debug=False)
    io = {}
    io["xTd"] = nc.dram_tensor("xTd", [NB * EMB, SEQ], F32, kind="ExternalInput").ap()
    io["w1td"] = nc.dram_tensor("w1td", [EMB, FF], F32, kind="ExternalInput").ap()
    io["w2td"] = nc.dram_tensor("w2td", [FF, EMB], F32, kind="ExternalInput").ap()
    io["wtriud"] = nc.dram_tensor("wtriud", [EMB, FF], F32, kind="ExternalInput").ap()
    io["btriud"] = nc.dram_tensor("btriud", [64, 64], F32, kind="ExternalInput").ap()
    io["wooTd"] = nc.dram_tensor("wooTd", [64, EMB], F32, kind="ExternalInput").ap()
    io["wviTd"] = nc.dram_tensor("wviTd", [128, 512], F32, kind="ExternalInput").ap()
    io["wvoTd"] = nc.dram_tensor("wvoTd", [64, EMB], F32, kind="ExternalInput").ap()
    io["cbd"] = nc.dram_tensor("cbd", [128, 128], F32, kind="ExternalInput").ap()
    io["b1d"] = nc.dram_tensor("b1d", [128, 32], F32, kind="ExternalInput").ap()
    io["b2d"] = nc.dram_tensor("b2d", [128, 8], F32, kind="ExternalInput").ap()
    io["nwd"] = nc.dram_tensor("nwd", [128, 8], F32, kind="ExternalInput").ap()
    io["bvi2d"] = nc.dram_tensor("bvi2d", [2, 64], F32, kind="ExternalInput").ap()
    io["bvo2d"] = nc.dram_tensor("bvo2d", [2, EMB], F32, kind="ExternalInput").ap()
    io["boo2d"] = nc.dram_tensor("boo2d", [2, EMB], F32, kind="ExternalInput").ap()
    io["mixd"] = nc.dram_tensor("mixd", [128, 2], F32, kind="ExternalInput").ap()
    io["identd"] = nc.dram_tensor("identd", [128, 128], F32, kind="ExternalInput").ap()
    io["onesrd"] = nc.dram_tensor("onesrd", [1, 128], F32, kind="ExternalInput").ap()
    io["onescbd"] = nc.dram_tensor("onescbd", [128, 1], BF16, kind="ExternalInput").ap()
    io["onescfd"] = nc.dram_tensor("onescfd", [128, 1], F32, kind="ExternalInput").ap()
    io["cpoold"] = nc.dram_tensor("cpoold", [128, 4], F32, kind="ExternalInput").ap()
    io["omdram"] = nc.dram_tensor("omdram", [NB, 4096], F32, kind="Internal").ap()
    io["outTd"] = nc.dram_tensor("outTd", [NB * EMB, SEQ], F32, kind="ExternalOutput").ap()
    if DEBUG_DUMP:
        _dtmap = {"FP8": FP8, "F32": F32, "BF16": BF16}
        for (nm, shape, dt) in DBG_SPECS:
            io["dbg_" + nm] = nc.dram_tensor("dbg_" + nm, shape, _dtmap[dt],
                                             kind="ExternalOutput").ap()
        for (nm, shape) in DBG_TW:
            io["dbg_" + nm] = nc.dram_tensor("dbg_" + nm, shape, F32,
                                             kind="ExternalOutput").ap()

    k1 = K(False)
    construct(k1, {n: Fake() for n in io})
    k2 = K(True, nc, M=k1.M)
    with k2.es:
        construct(k2, io)
    return nc


def kernel(L0_attn_x, L0_attn_omega_proj_w, L0_attn_omega_proj_b, L0_attn_omega_out_w,
           L0_attn_omega_out_b, L0_attn_vq_proj_in_w, L0_attn_vq_proj_in_b,
           L0_attn_vq_proj_out_w, L0_attn_vq_proj_out_b, L0_attn_codebook,
           L0_attn_omega_mix, L0_attn_vq_mix, L0_attn_norm_w, L0_ffn_W1, L0_ffn_b1,
           L0_ffn_W2, L0_ffn_b2):
    f32 = np.float32
    x = np.asarray(L0_attn_x, f32)
    xT = np.ascontiguousarray(x.transpose(0, 2, 1))
    w1t = np.ascontiguousarray(np.asarray(L0_ffn_W1, f32).T)
    w2t = np.ascontiguousarray(np.asarray(L0_ffn_W2, f32).T)

    opw = np.asarray(L0_attn_omega_proj_w, f32)
    opb = np.asarray(L0_attn_omega_proj_b, f32)
    R, C = np.triu_indices(64, k=1)
    wtriu = np.zeros((EMB, 64, 64), f32)
    wtriu[:, R, C] = opw.T
    wtriu = np.ascontiguousarray(wtriu.reshape(EMB, FF))
    btriu = np.zeros((64, 64), f32)
    btriu[R, C] = opb

    wooT = np.ascontiguousarray(np.asarray(L0_attn_omega_out_w, f32).T)
    wviT = np.asarray(L0_attn_vq_proj_in_w, f32).T
    wviT = np.ascontiguousarray(wviT.reshape(8, 128, 64).transpose(1, 0, 2).reshape(128, 512))
    wvoT = np.ascontiguousarray(np.asarray(L0_attn_vq_proj_out_w, f32).T)
    cb = np.asarray(L0_attn_codebook, f32)
    cbPP = np.ascontiguousarray(cb.reshape(2, 128, 64).transpose(1, 0, 2).reshape(128, 128))
    b1PP = np.ascontiguousarray(np.asarray(L0_ffn_b1, f32).reshape(32, 128).T)
    b2PP = np.ascontiguousarray(np.asarray(L0_ffn_b2, f32).reshape(8, 128).T)
    nwPP = np.ascontiguousarray(np.asarray(L0_attn_norm_w, f32).reshape(8, 128).T)
    bvi2 = np.ascontiguousarray(np.broadcast_to(np.asarray(L0_attn_vq_proj_in_b, f32), (2, 64)))
    bvo2 = np.ascontiguousarray(np.broadcast_to(np.asarray(L0_attn_vq_proj_out_b, f32), (2, EMB)))
    boo2 = np.ascontiguousarray(np.broadcast_to(np.asarray(L0_attn_omega_out_b, f32), (2, EMB)))
    mixd = np.empty((128, 2), f32)
    mixd[:, 0] = np.float32(L0_attn_omega_mix)
    mixd[:, 1] = np.float32(L0_attn_vq_mix)
    identd = np.eye(128, dtype=f32)
    onesrd = np.ones((1, 128), f32)
    onescbd = np.ones((128, 1), ml_dtypes.bfloat16)
    onescfd = np.ones((128, 1), f32)
    cpoold = np.broadcast_to(np.array([1e-6, 1e-5, 0.0, 0.0], f32), (128, 4)).copy()

    shared = {
        "w1td": w1t, "w2td": w2t, "wtriud": wtriu, "btriud": btriu, "wooTd": wooT,
        "wviTd": wviT, "wvoTd": wvoT, "cbd": cbPP, "b1d": b1PP, "b2d": b2PP,
        "nwd": nwPP, "bvi2d": bvi2, "bvo2d": bvo2, "boo2d": boo2, "mixd": mixd,
        "identd": identd, "onesrd": onesrd, "onescbd": onescbd, "onescfd": onescfd,
        "cpoold": cpoold,
    }
    in_maps = []
    for c in range(NCORES):
        m = dict(shared)
        m["xTd"] = np.ascontiguousarray(xT[2 * c:2 * c + 2].reshape(NB * EMB, SEQ))
        in_maps.append(m)

    nc = build_program()
    res = run_bass_kernel_spmd(nc, in_maps, core_ids=list(range(NCORES)))
    outs = []
    for c in range(NCORES):
        o = np.asarray(res.results[c]["outTd"], f32).reshape(NB, EMB, SEQ)
        outs.append(o.transpose(0, 2, 1))
    return np.ascontiguousarray(np.concatenate(outs, axis=0))


# version 04 (final)
# speedup vs baseline: 1.0081x; 1.0081x over previous
import numpy as np
import ml_dtypes
from contextlib import ExitStack, contextmanager

import concourse.bass as bass
import concourse.mybir as mybir
from concourse.bass_utils import run_bass_kernel_spmd
from concourse.alu_op_type import AluOpType as ALU

F32 = mybir.dt.float32
BF16 = mybir.dt.bfloat16
FP8 = mybir.dt.float8e4
AF = mybir.ActivationFunctionType
AX = mybir.AxisListType

EMB = 1024
FF = 4096
SEQ = 2048
NB = 2
NE = 8
NF = 32
T = 512
NT = 8
TPB = 4
NCORES = 8
MEAN_W = float(2.0 ** -22)
DEBUG_DUMP = False
DBG_SPECS = [("w1t8", [128, 8 * 4096], "FP8"), ("w2t8", [128, 32 * 1024], "FP8"),
             ("hm", [128, 16], "F32"), ("Gb", [64, 128], "BF16"),
             ("constPP", [128, 16], "F32"), ("woos", [64, 1024], "BF16"),
             ("hT", [128, 32 * 512], "BF16")]
DBG_TW = [("tpart", [128, 32]), ("tcomb", [128, 2]), ("trowS", [1, 128]),
          ("thrrow", [1, 4]), ("thrcol", [128, 2])]


class Fake:
    def __getattr__(self, n):
        return self

    def __call__(self, *a, **k):
        return self

    def __getitem__(self, i):
        return self

    def __enter__(self):
        return self

    def __exit__(self, *a):
        return False


class K:

    def __init__(self, emit, nc=None, M=None):
        self.emit = emit
        self.nc = nc
        self.cnt = {}
        self.M = M if M is not None else {}
        self.sems = {}
        self.es = ExitStack()

    def sem(self, name):
        self.cnt[name] = 0
        if self.emit:
            self.sems[name] = self.es.enter_context(self.nc.semaphore(name))

    def inc(self, instr, s, key=None, n=1):
        self.cnt[s] += n
        if key is not None:
            if not self.emit:
                self.M[(s,) + tuple(key)] = self.cnt[s]
        if self.emit and instr is not None:
            instr.then_inc(self.sems[s], n)

    def wait(self, eng, s, key):
        if self.emit:
            eng.wait_ge(self.sems[s], self.M[(s,) + tuple(key)])

    def dma(self, eng, semname, key, out, in_):
        cur = self.cnt[semname]
        if self.emit and cur:
            eng.wait_ge(self.sems[semname], cur)
        instr = eng.dma_start(out=out, in_=in_) if self.emit else None
        self.inc(instr, semname, key, 16)

    def sb(self, name, shape, dtype=F32):
        if self.emit:
            return self.es.enter_context(self.nc.sbuf_tensor(name, shape, dtype))
        return Fake()

    def ps(self, name, shape, dtype=F32):
        if self.emit:
            return self.es.enter_context(self.nc.psum_tensor(name, shape, dtype))
        return Fake()

    def scope(self):
        if self.emit:
            return self.nc_scope()
        return Fake()

    @contextmanager
    def nc_scope(self):
        with ExitStack() as inner:
            old = self.es
            self.es = inner
            try:
                yield
            finally:
                self.es = old

    @contextmanager
    def block(self, name):
        if self.emit:
            with self.nc.Block(name) as b:
                yield b
        else:
            yield Fake()

    def body(self, blk, engname, fn):
        if self.emit:
            getattr(blk, engname)(fn)
        else:
            fn(Fake())


def construct(k: "K", io):
    CONST_NAMES = ["identd", "onesrd", "onescbd", "onescfd", "cpoold", "b1d",
                   "b2d", "nwd", "mixd", "cbd", "wooTd", "wviTd", "wvoTd",
                   "btriud", "bvi2d", "bvo2d", "boo2d"]
    dma_sems = (["dq0", "dq1", "dq2", "dsw", "dow0", "dow1", "dor",
                 "dxl0", "dxl1", "dxr0", "dxr1", "do0", "do1", "dst0", "dst1"]
                + ["dc_" + n for n in CONST_NAMES])
    for s in ("t", "v", "s"):
        k.sem(s)
    for s in dma_sems:
        k.sem(s)

    w1t8 = k.sb("w1t8", [128, NE * FF], FP8)
    w2t8 = k.sb("w2t8", [128, NF * EMB], FP8)
    hm = k.sb("hm", [128, 16])
    hmv = k.sb("hmv", [128, 16])
    hmb = k.sb("hmb", [128, 16], BF16)
    Gb = k.sb("Gb", [64, NB * 64], BF16)
    constPP = k.sb("constPP", [128, 16])
    woos = k.sb("woos", [64, EMB], BF16)
    sident = k.sb("sident", [128, 128])
    sonesr = k.sb("sonesr", [1, 128])
    sonescb = k.sb("sonescb", [128, 1], BF16)
    sonescf = k.sb("sonescf", [128, 1])
    scpool = k.sb("scpool", [128, 4])
    sb1 = k.sb("sb1", [128, 32])
    sb2 = k.sb("sb2", [128, 8])
    snw = k.sb("snw", [128, 8])
    smix = k.sb("smix", [128, 2])
    scb = k.sb("scb", [128, 128])
    ncbT = k.sb("ncbT", [64, 256])
    soh = k.sb("soh", [128, 4])

    with k.scope():
        NPOOL = 3
        pool = [k.sb(f"pool{i}", [128, 2048]) for i in range(NPOOL)]
        sabs = [k.sb(f"sabs{i}", [128, 2048]) for i in range(2)]
        ssgn = [k.sb(f"ssgn{i}", [128, 2048], BF16) for i in range(2)]
        smsk = k.sb("smsk", [128, 2048], BF16)
        tpart = k.sb("tpart", [128, 32])
        tcomb = k.sb("tcomb", [128, 2])
        trowS = k.sb("trowS", [1, 128])
        thrrow = k.sb("thrrow", [1, 4])
        thrcol = k.sb("thrcol", [128, 2])
        m_row = [k.sb(f"m_row{b}", [1, 2048]) for b in range(NB)]
        msq_row = k.sb("msq_row", [1, 2048])
        rs_row = [k.sb(f"rs_row{b}", [1, 2048]) for b in range(NB)]
        rsb = k.sb("rsb", [128, 2048], BF16)
        mrb = k.sb("mrb", [128, 2048], BF16)
        womslice = [k.sb(f"womslice{i}", [2, 512]) for i in range(2)]
        omu = k.sb("omu", [64, 128])
        omg = k.sb("omg", [64, 128])
        m2s = k.sb("m2s", [64, 128])
        m3s = k.sb("m3s", [64, 128])
        gtmp = k.sb("gtmp", [64, 128])
        gtmp2 = k.sb("gtmp2", [64, 128])
        sbtriu = k.sb("sbtriu", [64, 64])
        wooT_t = k.sb("wooT_t", [64, EMB])
        wviT_t = k.sb("wviT_t", [128, 512])
        wvoT_t = k.sb("wvoT_t", [64, EMB])
        bvi2s = k.sb("bvi2s", [2, 64])
        bvo2s = k.sb("bvo2s", [2, EMB])
        boo2s = k.sb("boo2s", [2, EMB])
        vqi = k.sb("vqi", [2, 64])
        vqq = k.sb("vqq", [2, 64])
        vnorm = k.sb("vnorm", [2, 1])
        nvq = k.sb("nvq", [2, 64])
        nvqT = k.sb("nvqT", [64, 2])
        csq = k.sb("csq", [128, 128])
        cnorm = k.sb("cnorm", [128, 2])
        sncb = k.sb("sncb", [128, 128])
        scoresR = k.sb("scoresR", [2, 256])
        mxr = k.sb("mxr", [2, 1])
        mxT = k.sb("mxT", [1, 2])
        scT = k.sb("scT", [128, 4])
        vqcT = k.sb("vqcT", [64, 2])
        vcrow = k.sb("vcrow", [2, EMB])
        c2row = k.sb("c2row", [2, EMB])
        psB = [k.ps(f"psB{i}", [128, 512]) for i in range(8)]

        def w_chunk_src(w, c):
            if w == 0:
                e, h = c // 2, c % 2
                return io["w1td"][e * 128:(e + 1) * 128, h * 2048:(h + 1) * 2048]
            return io["w2td"][c * 256:(c + 1) * 256, :].rearrange("(j p) c -> p j c", p=128)

        def xa_src(b, e):
            return io["xTd"][b * EMB + e * 128: b * EMB + (e + 1) * 128, :]

        def wtriu_src(c):
            e, h = c // 2, c % 2
            return io["wtriud"][e * 128:(e + 1) * 128, h * 2048:(h + 1) * 2048]

        uses = []
        for c in range(16):
            uses.append(("w1a", c, ("v", ("twred", 0, c))))
        for b in range(NB):
            for e in range(NE):
                uses.append(("xa1", (b, e), ("t", ("pamm", b, e))))
        for c in range(16):
            uses.append(("w1b", c, ("s", ("twsgn", 0, c))))
        for b in range(NB):
            for e in range(NE):
                uses.append(("xa2", (b, e), ("v", ("paxn", b, e))))
        for c in range(16):
            uses.append(("wtr", c, ("t", ("spom", c))))
        for c in range(16):
            uses.append(("w2a", c, ("v", ("twred", 1, c))))
        for c in range(16):
            uses.append(("w2b", c, ("s", ("twsgn", 1, c))))
        POOLBUF = {}
        for u, (stream, idx, _cons) in enumerate(uses):
            POOLBUF[(stream, idx)] = u % NPOOL

        sabs_uses = []
        for b in range(NB):
            for e in range(NE):
                sabs_uses.append(("a1", (b, e), ("t", ("pamm", b, e))))
        for w in range(2):
            for c in range(16):
                sabs_uses.append(("wp2", (w, c), ("v", ("twt8", w, c))))
        SABSBUF = {}
        for u, (stream, idx, _cons) in enumerate(sabs_uses):
            SABSBUF[(stream, idx)] = u % 2

        with k.block("ph0") as blk:
            def sync_body(sync):
                cdst = {"identd": sident, "onesrd": sonesr, "onescbd": sonescb,
                        "onescfd": sonescf, "cpoold": scpool, "b1d": sb1,
                        "b2d": sb2, "nwd": snw, "mixd": smix, "cbd": scb,
                        "wooTd": wooT_t, "wviTd": wviT_t, "wvoTd": wvoT_t,
                        "btriud": sbtriu, "bvi2d": bvi2s, "bvo2d": bvo2s,
                        "boo2d": boo2s}
                for dname in CONST_NAMES:
                    k.dma(sync, "dc_" + dname, ("c", dname), out=cdst[dname][:],
                          in_=io[dname])
                srcs = {"w1a": lambda c: w_chunk_src(0, c),
                        "w1b": lambda c: w_chunk_src(0, c),
                        "w2a": lambda c: w_chunk_src(1, c),
                        "w2b": lambda c: w_chunk_src(1, c),
                        "xa1": lambda be: xa_src(*be),
                        "xa2": lambda be: xa_src(*be),
                        "wtr": wtriu_src}
                def om_bounce():
                    for s in range(8):
                        k.wait(sync, "v", ("spomev", s))
                        for b in range(NB):
                            k.dma(sync, f"dow{b}", ("spomw", s, b),
                                  out=io["omdram"][b:b + 1, s * 512:(s + 1) * 512],
                                  in_=womslice[s % 2][b:b + 1, :])
                    k.wait(sync, "dow0", ("spomw", 7, 0))
                    k.wait(sync, "dow1", ("spomw", 7, 1))
                    for b in range(NB):
                        rdsrc = io["omdram"][b:b + 1, :].rearrange("one (r c) -> (one r) c",
                                                                   r=64)
                        k.dma(sync, "dor", ("spomr", b), out=omu[:, b * 64:(b + 1) * 64],
                              in_=rdsrc)
                for u, (stream, idx, _cons) in enumerate(uses):
                    if (stream, idx) == ("w2a", 0):
                        om_bounce()
                    if u >= NPOOL:
                        ps_, pk = uses[u - NPOOL][2]
                        k.wait(sync, ps_, pk)
                    buf = pool[u % NPOOL][:]
                    if stream in ("w2a", "w2b"):
                        buf = buf.rearrange("p (j c) -> p j c", j=2)
                    k.dma(sync, f"dq{u % NPOOL}", (stream, idx), out=buf,
                          in_=srcs[stream](idx))
                    if (stream, idx) == ("xa2", (0, 7)):
                        k.wait(sync, "v", ("parows", 0))
                        k.dma(sync, "dsw", ("sw", 0, 0), out=io["stats_d"][0:1, :],
                              in_=m_row[0][:])
                        k.dma(sync, "dsw", ("sw", 0, 1), out=io["stats_d"][1:2, :],
                              in_=rs_row[0][:])
                    if (stream, idx) == ("xa2", (1, 7)):
                        k.wait(sync, "v", ("parows", 1))
                        k.dma(sync, "dsw", ("sw", 1, 0), out=io["stats_d"][2:3, :],
                              in_=m_row[1][:])
                        k.dma(sync, "dsw", ("sw", 1, 1), out=io["stats_d"][3:4, :],
                              in_=rs_row[1][:])
                for dname in CONST_NAMES:
                    k.wait(sync, "dc_" + dname, ("c", dname))
                for i in range(NPOOL):
                    last = [u for u in range(len(uses)) if u % NPOOL == i][-1]
                    k.wait(sync, f"dq{i}", uses[last][0:2])
                k.wait(sync, "dsw", ("sw", 1, 1))
                k.wait(sync, "dor", ("spomr", 1))
                if DEBUG_DUMP:
                    k.wait(sync, "v", ("twthrc", 1))
                    twd = {"tpart": tpart, "tcomb": tcomb, "trowS": trowS,
                           "thrrow": thrrow, "thrcol": thrcol}
                    for (nm, _shape) in DBG_TW:
                        k.dma(sync, "dow0", ("dbgtw", nm), out=io["dbg_" + nm],
                              in_=twd[nm][:])
                    k.wait(sync, "dow0", ("dbgtw", DBG_TW[-1][0]))
            k.body(blk, "sync", sync_body)

            def tensor_body(tensor):
                k.wait(tensor, "dc_identd", ("c", "identd"))
                k.wait(tensor, "dc_onesrd", ("c", "onesrd"))
                k.wait(tensor, "dc_onescfd", ("c", "onescfd"))
                k.wait(tensor, "v", ("twcomb", 0))
                k.inc(tensor.transpose(psB[0][0:1, 0:128], tcomb[:, 0:1], sident[:]),
                      "t", ("twtr", 0))
                k.wait(tensor, "v", ("twthr", 0))
                k.inc(tensor.matmul(psB[1][:, 0:1], lhsT=sonesr[:],
                                    rhs=thrrow[0:1, 1:2], start=True, stop=True),
                      "t", ("twbc", 0))
                for b in range(NB):
                    for e in range(NE):
                        k.wait(tensor, "s", ("pasq", b, e))
                        if b == 0 and e == 0:
                            k.wait(tensor, "v", ("twthrc", 0))
                        if b == 1 and e == 0:
                            k.wait(tensor, "v", ("bcev", 0))
                        buf = pool[POOLBUF[("xa1", (b, e))]]
                        sq = sabs[SABSBUF[("a1", (b, e))]]
                        for s in range(4):
                            tensor.matmul(psB[s][0:1, :], lhsT=sonescf[:],
                                          rhs=buf[:, s * 512:(s + 1) * 512],
                                          start=(e == 0), stop=(e == NE - 1))
                        for s in range(4):
                            mm = tensor.matmul(psB[4 + s][0:1, :], lhsT=sonescf[:],
                                               rhs=sq[:, s * 512:(s + 1) * 512],
                                               start=(e == 0), stop=(e == NE - 1))
                        k.inc(mm, "t", ("pamm", b, e))
                    k.wait(tensor, "v", ("parows", b))
                    for s in range(4):
                        tensor.matmul(psB[s][:, :], lhsT=sonesr[:],
                                      rhs=rs_row[b][0:1, s * 512:(s + 1) * 512],
                                      start=True, stop=True)
                    for s in range(4):
                        mm = tensor.matmul(psB[4 + s][:, :], lhsT=sonesr[:],
                                           rhs=m_row[b][0:1, s * 512:(s + 1) * 512],
                                           start=True, stop=True)
                    k.inc(mm, "t", ("pabc", b))
                k.wait(tensor, "v", ("pahmv",))
                k.wait(tensor, "v", ("bcev", 1))
                for c in range(16):
                    e, h = c // 2, c % 2
                    k.wait(tensor, f"dq{POOLBUF[('wtr', c)]}", ("wtr", c))
                    buf = pool[POOLBUF[("wtr", c)]]
                    for s4 in range(4):
                        mm = tensor.matmul(psB[h * 4 + s4][0:2, :],
                                           lhsT=hmv[:, 2 * e:2 * e + 2],
                                           rhs=buf[:, s4 * 512:(s4 + 1) * 512],
                                           start=(e == 0), stop=(e == 7))
                    k.inc(mm, "t", ("spom", c))
                k.wait(tensor, "v", ("twcomb", 1))
                k.wait(tensor, "v", ("spomev", 7))
                k.inc(tensor.transpose(psB[0][0:1, 0:128], tcomb[:, 1:2], sident[:]),
                      "t", ("twtr", 1))
                k.wait(tensor, "v", ("twthr", 1))
                k.inc(tensor.matmul(psB[1][:, 0:1], lhsT=sonesr[:],
                                    rhs=thrrow[0:1, 3:4], start=True, stop=True),
                      "t", ("twbc", 1))
                p_a, p_b2, p_c = psB[2], psB[3], psB[4]
                for b in range(NB):
                    k.wait(tensor, "v", ("spomu", b))
                    k.inc(tensor.transpose(p_a[0:64, 0:64], omu[:, b * 64:(b + 1) * 64],
                                           sident[0:64, 0:64]), "t", ("sptr", b))
                    k.wait(tensor, "v", ("spomb", b))
                    k.inc(tensor.matmul(p_b2[0:64, 0:64], lhsT=omg[:, b * 64:(b + 1) * 64],
                                        rhs=omg[:, b * 64:(b + 1) * 64], start=True, stop=True),
                          "t", ("spm2", b))
                    k.wait(tensor, "v", ("spm2e", b))
                    k.inc(tensor.matmul(p_c[0:64, 0:64], lhsT=omg[:, b * 64:(b + 1) * 64],
                                        rhs=m2s[:, b * 64:(b + 1) * 64], start=True, stop=True),
                          "t", ("spm3", b))
                k.wait(tensor, "v", ("spG", 1))
                k.wait(tensor, "dc_wviTd", ("c", "wviTd"))
                for e in range(NE):
                    mm = tensor.matmul(p_a[0:2, 0:64], lhsT=hmv[:, 2 * e:2 * e + 2],
                                       rhs=wviT_t[:, e * 64:(e + 1) * 64],
                                       start=(e == 0), stop=(e == 7))
                k.inc(mm, "t", ("spvq",))
                for c in range(2):
                    k.wait(tensor, "v", ("spncb", c))
                    if c == 1:
                        k.wait(tensor, "v", ("spncbtev", 0))
                    k.inc(tensor.transpose(p_b2[0:64, 0:128], sncb[:, c * 64:(c + 1) * 64],
                                           sident[:]), "t", ("spncbt", c))
                k.wait(tensor, "v", ("spnvq",))
                k.inc(tensor.transpose(p_c[0:64, 0:2], nvq[:], sident[0:2, 0:2]),
                      "t", ("spnvqt",))
                k.wait(tensor, "v", ("spncbtev", 1))
                k.wait(tensor, "v", ("spnvqte",))
                k.inc(tensor.matmul(p_a[0:2, 0:256], lhsT=nvqT[:], rhs=ncbT[:],
                                    start=True, stop=True), "t", ("spsc",))
                k.wait(tensor, "v", ("spmx",))
                k.inc(tensor.transpose(p_b2[0:1, 0:2], mxr[:], sident[0:2, 0:2]),
                      "t", ("spmxt",))
                k.wait(tensor, "v", ("spmxte",))
                k.inc(tensor.matmul(p_c[:, 0:2], lhsT=sonesr[:], rhs=mxT[:],
                                    start=True, stop=True), "t", ("spmb",))
                for c in range(2):
                    k.inc(tensor.transpose(p_a[:, c * 2:c * 2 + 2],
                                           scoresR[:, c * 128:(c + 1) * 128],
                                           sident[0:2, 0:2]), "t", ("spsct", c))
                k.wait(tensor, "v", ("spoh", 1))
                for c in range(2):
                    mm = tensor.matmul(p_b2[0:64, 0:2], lhsT=scb[:, c * 64:(c + 1) * 64],
                                       rhs=soh[:, c * 2:c * 2 + 2],
                                       start=(c == 0), stop=(c == 1))
                k.inc(mm, "t", ("spvc",))
                k.wait(tensor, "v", ("spvce",))
                k.wait(tensor, "dc_wvoTd", ("c", "wvoTd"))
                for s in range(2):
                    k.inc(tensor.matmul(p_a[0:2, 0:512], lhsT=vqcT[:],
                                        rhs=wvoT_t[:, s * 512:(s + 1) * 512],
                                        start=True, stop=True), "t", ("spvcr", s))
                    k.wait(tensor, "v", ("spvcre", s))
                k.wait(tensor, "v", ("spc2",))
                for e in range(NE):
                    k.inc(tensor.transpose(p_c[:, e * 2:e * 2 + 2],
                                           c2row[:, e * 128:(e + 1) * 128],
                                           sident[0:2, 0:2]), "t", ("spct", e))
            k.body(blk, "tensor", tensor_body)

            def vector_body(vector):
                for c in range(16):
                    k.wait(vector, f"dq{POOLBUF[('w1a', c)]}", ("w1a", c))
                    k.inc(vector.reduce_sum(tpart[:, c:c + 1],
                                            pool[POOLBUF[("w1a", c)]][:], axis=AX.X,
                                            apply_absolute_value=True),
                          "v", ("twred", 0, c))

                def thr_chain(w):
                    k.wait(vector, "v", ("twred", w, 15))
                    k.inc(vector.reduce_sum(tcomb[:, w:w + 1], tpart[:, w * 16:w * 16 + 16],
                                            axis=AX.X), "v", ("twcomb", w))
                    k.wait(vector, "v", ("twcomb", w))
                    k.wait(vector, "t", ("twtr", w))
                    k.inc(vector.tensor_copy(trowS[:], psB[0][0:1, 0:128]), "v", ("twcp", w))
                    k.wait(vector, "v", ("twcp", w))
                    k.inc(vector.reduce_sum(thrrow[0:1, w * 2:w * 2 + 1], trowS[:],
                                            axis=AX.X), "v", ("twsum", w))
                    k.wait(vector, "v", ("twsum", w))
                    k.inc(vector.tensor_scalar(thrrow[0:1, w * 2 + 1:w * 2 + 2],
                                               thrrow[0:1, w * 2:w * 2 + 1],
                                               scalar1=MEAN_W, scalar2=1e-8,
                                               op0=ALU.mult, op1=ALU.max),
                          "v", ("twthr", w))
                    k.wait(vector, "t", ("twbc", w))
                    k.inc(vector.tensor_scalar(thrcol[:, w:w + 1], psB[1][:, 0:1],
                                               scalar1=0.5, scalar2=None, op0=ALU.mult),
                          "v", ("twthrc", w))
                thr_chain(0)

                def pa_rows(b):
                    k.wait(vector, "t", ("pamm", b, NE - 1))
                    for s in range(4):
                        vector.tensor_scalar(m_row[b][0:1, s * 512:(s + 1) * 512],
                                             psB[s][0:1, :], scalar1=1.0 / EMB,
                                             scalar2=None, op0=ALU.mult)
                    for s in range(4):
                        ts_ev = vector.tensor_scalar(msq_row[0:1, s * 512:(s + 1) * 512],
                                                     psB[4 + s][0:1, :], scalar1=1.0 / EMB,
                                                     scalar2=None, op0=ALU.mult)
                    k.inc(ts_ev, "v", ("paev", b))
                    k.wait(vector, "v", ("paev", b))
                    vector.tensor_tensor(rs_row[b][:], m_row[b][:], m_row[b][:], op=ALU.mult)
                    k.inc(vector.tensor_tensor(msq_row[:], msq_row[:], rs_row[b][:],
                                               op=ALU.subtract), "v", ("pavar", b))
                    k.wait(vector, "s", ("pasqrt", b))
                    k.inc(vector.reciprocal(rs_row[b][:], msq_row[:]), "v", ("parecip", b))
                    k.wait(vector, "v", ("parecip", b))
                    k.inc(vector.tensor_tensor(m_row[b][:], m_row[b][:], rs_row[b][:],
                                               op=ALU.mult), "v", ("parows", b))

                def pa_bcev(b):
                    k.wait(vector, "t", ("pabc", b))
                    for s in range(4):
                        vector.tensor_copy(rsb[:, s * 512:(s + 1) * 512], psB[s][:, :])
                    for s in range(4):
                        cp = vector.tensor_copy(mrb[:, s * 512:(s + 1) * 512], psB[4 + s][:, :])
                    k.inc(cp, "v", ("bcev", b))

                def pa_a2(b):
                    for e in range(NE):
                        k.wait(vector, f"dq{POOLBUF[('xa2', (b, e))]}", ("xa2", (b, e)))
                        buf = pool[POOLBUF[("xa2", (b, e))]]
                        vector.tensor_tensor(sabs[0][:], buf[:], rsb[:], op=ALU.mult)
                        vector.tensor_tensor(smsk[:], sabs[0][:], mrb[:], op=ALU.subtract)
                        k.inc(vector.reduce_sum(hm[:, e * 2 + b:e * 2 + b + 1], smsk[:],
                                                axis=AX.X), "v", ("paxn", b, e))

                pa_rows(0)
                pa_bcev(0)
                pa_rows(1)
                for c in range(16):
                    k.wait(vector, "s", ("twabs", 0, c))
                    vector.tensor_scalar(smsk[:], sabs[SABSBUF[("wp2", (0, c))]][:],
                                         scalar1=thrcol[:, 0:1], scalar2=None, op0=ALU.is_ge)
                    k.inc(vector.tensor_tensor(w1t8[:, c * 2048:(c + 1) * 2048], smsk[:],
                                               ssgn[SABSBUF[("wp2", (0, c))]][:], op=ALU.mult),
                          "v", ("twt8", 0, c))
                pa_a2(0)
                pa_bcev(1)
                pa_a2(1)
                k.wait(vector, "v", ("paxn", 1, 7))
                k.inc(vector.tensor_scalar(hmv[:], hm[:], scalar1=1.0 / SEQ, scalar2=None,
                                           op0=ALU.mult), "v", ("pahmv",))
                k.inc(vector.tensor_copy(hmb[:], hmv[:]), "v", ("pahmb",))
                for s in range(8):
                    k.wait(vector, "t", ("spom", 15))
                    if s >= 2:
                        k.wait(vector, "dow0", ("spomw", s - 2, 0))
                        k.wait(vector, "dow1", ("spomw", s - 2, 1))
                    k.inc(vector.tensor_copy(womslice[s % 2][:], psB[s][0:2, :]),
                          "v", ("spomev", s))
                for c in range(16):
                    k.wait(vector, f"dq{POOLBUF[('w2a', c)]}", ("w2a", c))
                    k.inc(vector.reduce_sum(tpart[:, 16 + c:16 + c + 1],
                                            pool[POOLBUF[("w2a", c)]][:], axis=AX.X,
                                            apply_absolute_value=True),
                          "v", ("twred", 1, c))
                thr_chain(1)
                for c in range(16):
                    k.wait(vector, "s", ("twabs", 1, c))
                    vector.tensor_scalar(smsk[:], sabs[SABSBUF[("wp2", (1, c))]][:],
                                         scalar1=thrcol[:, 1:2], scalar2=None, op0=ALU.is_ge)
                    k.inc(vector.tensor_tensor(w2t8[:, c * 2048:(c + 1) * 2048], smsk[:],
                                               ssgn[SABSBUF[("wp2", (1, c))]][:], op=ALU.mult),
                          "v", ("twt8", 1, c))
                for b in range(NB):
                    k.wait(vector, "dor", ("spomr", b))
                    k.wait(vector, "dc_btriud", ("c", "btriud"))
                    k.inc(vector.tensor_tensor(omu[:, b * 64:(b + 1) * 64],
                                               omu[:, b * 64:(b + 1) * 64], sbtriu[:],
                                               op=ALU.add), "v", ("spomu", b))
                    k.wait(vector, "t", ("sptr", b))
                    k.inc(vector.tensor_tensor(omg[:, b * 64:(b + 1) * 64],
                                               omu[:, b * 64:(b + 1) * 64],
                                               psB[2][0:64, 0:64], op=ALU.subtract),
                          "v", ("spomb", b))
                    k.wait(vector, "t", ("spm2", b))
                    k.inc(vector.tensor_copy(m2s[:, b * 64:(b + 1) * 64], psB[3][0:64, 0:64]),
                          "v", ("spm2e", b))
                    k.wait(vector, "v", ("spm2e", b))
                    k.wait(vector, "t", ("spm3", b))
                    k.inc(vector.tensor_copy(m3s[:, b * 64:(b + 1) * 64], psB[4][0:64, 0:64]),
                          "v", ("spm3e", b))
                    k.wait(vector, "v", ("spm3e", b))
                    bs = slice(b * 64, (b + 1) * 64)
                    vector.tensor_scalar(gtmp[:, bs], m2s[:, bs], scalar1=-0.5, scalar2=None,
                                         op0=ALU.mult)
                    vector.tensor_tensor(gtmp[:, bs], gtmp[:, bs], omg[:, bs], op=ALU.add)
                    vector.tensor_scalar(gtmp2[:, bs], m3s[:, bs], scalar1=1.0 / 6.0,
                                         scalar2=None, op0=ALU.mult)
                    vector.tensor_tensor(gtmp[:, bs], gtmp[:, bs], gtmp2[:, bs], op=ALU.add)
                    vector.tensor_tensor(gtmp[:, bs], gtmp[:, bs], sident[0:64, 0:64],
                                         op=ALU.add)
                    k.inc(vector.tensor_copy(Gb[:, bs], gtmp[:, bs]), "v", ("spG", b))
                k.wait(vector, "t", ("spvq",))
                k.wait(vector, "dc_bvi2d", ("c", "bvi2d"))
                k.inc(vector.tensor_tensor(vqi[:], psB[2][0:2, 0:64], bvi2s[:], op=ALU.add),
                      "v", ("spvqi",))
                k.wait(vector, "s", ("spvqsq",))
                k.inc(vector.reduce_sum(vnorm[:], vqq[:], axis=AX.X), "v", ("spvnr",))
                k.wait(vector, "s", ("spvqn",))
                vector.tensor_scalar(vnorm[:], vnorm[:], scalar1=1e-12, scalar2=None,
                                     op0=ALU.max)
                k.inc(vector.reciprocal(vnorm[:], vnorm[:]), "v", ("spvrec",))
                k.wait(vector, "v", ("spvrec",))
                k.inc(vector.tensor_scalar(nvq[:], vqi[:], scalar1=vnorm[:], scalar2=None,
                                           op0=ALU.mult), "v", ("spnvq",))
                k.wait(vector, "s", ("spcbsq",))
                for c in range(2):
                    rs_cn = vector.reduce_sum(cnorm[:, c:c + 1], csq[:, c * 64:(c + 1) * 64],
                                              axis=AX.X)
                k.inc(rs_cn, "v", ("spcn",))
                k.wait(vector, "s", ("spcbn",))
                for c in range(2):
                    vector.tensor_scalar(cnorm[:, c:c + 1], cnorm[:, c:c + 1], scalar1=1e-12,
                                         scalar2=None, op0=ALU.max)
                k.inc(vector.reciprocal(cnorm[:], cnorm[:]), "v", ("spcrec",))
                k.wait(vector, "v", ("spcrec",))
                for c in range(2):
                    k.inc(vector.tensor_scalar(sncb[:, c * 64:(c + 1) * 64],
                                               scb[:, c * 64:(c + 1) * 64],
                                               scalar1=cnorm[:, c:c + 1], scalar2=None,
                                               op0=ALU.mult), "v", ("spncb", c))
                for c in range(2):
                    k.wait(vector, "t", ("spncbt", c))
                    k.inc(vector.tensor_copy(ncbT[:, c * 128:(c + 1) * 128],
                                             psB[3][0:64, 0:128]), "v", ("spncbtev", c))
                k.wait(vector, "t", ("spnvqt",))
                k.inc(vector.tensor_copy(nvqT[:], psB[4][0:64, 0:2]), "v", ("spnvqte",))
                k.wait(vector, "t", ("spsc",))
                vector.tensor_copy(scoresR[:], psB[2][0:2, 0:256])
                k.inc(vector.reduce_max(mxr[:], scoresR[:], axis=AX.X), "v", ("spmx",))
                k.wait(vector, "t", ("spmxt",))
                k.inc(vector.tensor_copy(mxT[:], psB[3][0:1, 0:2]), "v", ("spmxte",))
                k.wait(vector, "t", ("spsct", 1))
                for c in range(2):
                    vector.tensor_copy(scT[:, c * 2:c * 2 + 2], psB[2][:, c * 2:c * 2 + 2])
                k.wait(vector, "t", ("spmb",))
                for c in range(2):
                    k.inc(vector.tensor_tensor(soh[:, c * 2:c * 2 + 2],
                                               scT[:, c * 2:c * 2 + 2], psB[4][:, 0:2],
                                               op=ALU.is_ge), "v", ("spoh", c))
                k.wait(vector, "t", ("spvc",))
                k.inc(vector.tensor_copy(vqcT[:], psB[3][0:64, 0:2]), "v", ("spvce",))
                k.wait(vector, "dc_bvo2d", ("c", "bvo2d"))
                k.wait(vector, "dc_boo2d", ("c", "boo2d"))
                for s in range(2):
                    k.wait(vector, "t", ("spvcr", s))
                    k.inc(vector.tensor_tensor(vcrow[:, s * 512:(s + 1) * 512],
                                               psB[2][0:2, 0:512],
                                               bvo2s[:, s * 512:(s + 1) * 512],
                                               op=ALU.add), "v", ("spvcre", s))
                vector.tensor_scalar(vcrow[:], vcrow[:], scalar1=smix[0:2, 1:2], scalar2=None,
                                     op0=ALU.mult)
                vector.tensor_scalar(c2row[:], boo2s[:], scalar1=smix[0:2, 0:1], scalar2=None,
                                     op0=ALU.mult)
                k.inc(vector.tensor_tensor(c2row[:], c2row[:], vcrow[:], op=ALU.add),
                      "v", ("spc2",))
                k.wait(vector, "t", ("spct", NE - 1))
                for e in range(NE):
                    cp_c = vector.tensor_copy(constPP[:, e * 2:e * 2 + 2],
                                              psB[4][:, e * 2:e * 2 + 2])
                k.inc(cp_c, "v", ("spcpp",))
                k.wait(vector, "dc_wooTd", ("c", "wooTd"))
                k.wait(vector, "dc_mixd", ("c", "mixd"))
                k.inc(vector.tensor_scalar(woos[:], wooT_t[:], scalar1=smix[0:64, 0:1],
                                           scalar2=None, op0=ALU.mult), "v", ("spwoos",))
            k.body(blk, "vector", vector_body)

            def scalar_body(scalar):
                for b in range(NB):
                    for e in range(NE):
                        k.wait(scalar, f"dq{POOLBUF[('xa1', (b, e))]}", ("xa1", (b, e)))
                        u = b * 8 + e
                        if u >= 2:
                            pu = u - 2
                            k.wait(scalar, "t", ("pamm", pu // 8, pu % 8))
                        k.inc(scalar.square(sabs[SABSBUF[("a1", (b, e))]][:],
                                            pool[POOLBUF[("xa1", (b, e))]][:]),
                              "s", ("pasq", b, e))
                    k.wait(scalar, "v", ("pavar", b))
                    k.inc(scalar.activation(msq_row[:], msq_row[:], AF.Sqrt,
                                            bias=scpool[0:1, 1:2]), "s", ("pasqrt", b))
                for w in range(2):
                    for c in range(16):
                        stream = "w1b" if w == 0 else "w2b"
                        k.wait(scalar, f"dq{POOLBUF[(stream, c)]}", (stream, c))
                        k.wait(scalar, "v", ("twthrc", w))
                        su = SABSBUF[("wp2", (w, c))]
                        gu = 16 + w * 16 + c
                        if gu >= 2:
                            ps_, pk = sabs_uses[gu - 2][2]
                            k.wait(scalar, ps_, pk)
                        buf = pool[POOLBUF[(stream, c)]]
                        k.inc(scalar.activation(sabs[su][:], buf[:], AF.Abs),
                              "s", ("twabs", w, c))
                        k.inc(scalar.sign(ssgn[su][:], buf[:]), "s", ("twsgn", w, c))
                k.wait(scalar, "v", ("spvqi",))
                k.inc(scalar.square(vqq[:], vqi[:]), "s", ("spvqsq",))
                k.wait(scalar, "dc_cbd", ("c", "cbd"))
                k.inc(scalar.square(csq[:], scb[:]), "s", ("spcbsq",))
                k.wait(scalar, "v", ("spvnr",))
                k.inc(scalar.activation(vnorm[:], vnorm[:], AF.Sqrt,
                                        bias=scpool[0:2, 2:3]), "s", ("spvqn",))
                k.wait(scalar, "v", ("spcn",))
                k.inc(scalar.activation(cnorm[:], cnorm[:], AF.Sqrt,
                                        bias=scpool[:, 2:3]), "s", ("spcbn",))
            k.body(blk, "scalar", scalar_body)

    with k.scope():
        hT = k.sb("hT", [128, NF * T], BF16)
        xnt = [k.sb(f"xnt{i}", [128, NE * T], BF16) for i in range(2)]
        attn = k.sb("attn", [128, NE * T], BF16)
        xr_sb = k.sb("xr_sb", [64, T], BF16)
        abuf = [k.sb(f"abuf{i}", [128, T]) for i in range(2)]
        ubuf = [k.sb(f"ubuf{i}", [128, T]) for i in range(2)]
        sqb = [k.sb(f"sqb{i}", [128, T], BF16) for i in range(NE)]
        gtb = [k.sb(f"gtb{i}", [128, T], BF16) for i in range(2)]
        outb = [k.sb(f"outb{i}", [128, T]) for i in range(2)]
        xlnb = [k.sb(f"xlnb{i}", [128, T]) for i in range(2)]
        xresb = [k.sb(f"xresb{i}", [128, T]) for i in range(2)]
        rms_row = k.sb("rms_row", [1, T])
        rstd_row = k.sb("rstd_row", [1, T])
        mrbuf = [k.sb(f"mrbuf{i}", [1, T]) for i in range(2)]
        rsbuf = [k.sb(f"rsbuf{i}", [1, T]) for i in range(2)]

        p_mm = [k.ps(f"p_mm{i}", [128, T]) for i in range(2)]
        p_mr = k.ps("p_mr", [128, T])
        p_rs = k.ps("p_rs", [128, T])
        p_rstd = k.ps("p_rstd", [128, T])
        p_lie = [k.ps(f"p_lie{i}", [128, T]) for i in range(2)]
        p_ms = k.ps("p_ms", [1, T])

        def x_src(t, e):
            b, i = t // TPB, t % TPB
            return io["xTd"][b * EMB + e * 128: b * EMB + (e + 1) * 128,
                             i * T:(i + 1) * T]

        def out_dst(t, e):
            b, i = t // TPB, t % TPB
            return io["outTd"][b * EMB + e * 128: b * EMB + (e + 1) * 128,
                               i * T:(i + 1) * T]

        def stats_src(t, row):
            b, i = t // TPB, t % TPB
            return io["stats_d"][b * 2 + row: b * 2 + row + 1, i * T:(i + 1) * T]

        with k.block("pb") as blk:
            def sync_body(sync):
                for t in (0, 1):
                    k.dma(sync, "dst0", ("stm", t), out=mrbuf[t % 2][:], in_=stats_src(t, 0))
                    k.dma(sync, "dst1", ("str", t), out=rsbuf[t % 2][:], in_=stats_src(t, 1))
                for e in range(NE):
                    if e >= 2:
                        k.wait(sync, "v", ("xn", 0, e - 2))
                    k.dma(sync, f"dxl{e % 2}", ("xln", 0, e),
                          out=xlnb[e % 2][:], in_=x_src(0, e))
                for t in range(NT):
                    if t + 2 < NT:
                        k.wait(sync, "t", ("bc", t))
                        k.dma(sync, "dst0", ("stm", t + 2), out=mrbuf[t % 2][:],
                              in_=stats_src(t + 2, 0))
                        k.dma(sync, "dst1", ("str", t + 2), out=rsbuf[t % 2][:],
                              in_=stats_src(t + 2, 1))
                    if t + 1 < NT:
                        for e in range(NE):
                            pe, pt = e - 2, t + 1
                            if pe < 0:
                                pe += 8
                                pt -= 1
                            k.wait(sync, "v", ("xn", pt, pe))
                            k.dma(sync, f"dxl{e % 2}", ("xln", t + 1, e),
                                  out=xlnb[e % 2][:], in_=x_src(t + 1, e))
                    for e in range(NE):
                        if (t, e) != (0, 0) and (t, e) != (0, 1):
                            pe = e - 2
                            pt = t
                            if pe < 0:
                                pe += 8
                                pt -= 1
                            k.wait(sync, "v", ("sg", pt, pe))
                        k.dma(sync, f"dxr{e % 2}", ("xres", t, e),
                              out=xresb[e % 2][:], in_=x_src(t, e))
                        if e >= 2:
                            k.wait(sync, "v", ("sg", t, e - 2))
                            k.dma(sync, f"do{e % 2}", ("od", t, e - 2),
                                  out=out_dst(t, e - 2), in_=outb[e % 2][:])
                    for e in (6, 7):
                        k.wait(sync, "v", ("sg", t, e))
                        k.dma(sync, f"do{e % 2}", ("od", t, e),
                              out=out_dst(t, e), in_=outb[e % 2][:])
                k.wait(sync, "do0", ("od", NT - 1, NE - 2))
                k.wait(sync, "do1", ("od", NT - 1, NE - 1))
                if DEBUG_DUMP:
                    dbg_tensors = {"w1t8": w1t8, "hm": hm, "Gb": Gb, "w2t8": w2t8,
                                   "constPP": constPP, "woos": woos, "hT": hT}
                    for (nm, _shape, _dt) in DBG_SPECS:
                        k.dma(sync, "do0", ("dbg", nm), out=io["dbg_" + nm],
                              in_=dbg_tensors[nm][:])
                    k.wait(sync, "do0", ("dbg", DBG_SPECS[-1][0]))
            k.body(blk, "sync", sync_body)

            def tensor_body(tensor):
                k.wait(tensor, "dst0", ("stm", 0))
                k.wait(tensor, "dst1", ("str", 0))
                tensor.matmul(p_mr[:], lhsT=sonesr[:], rhs=mrbuf[0][:],
                              start=True, stop=True)
                mm = tensor.matmul(p_rs[:], lhsT=sonesr[:], rhs=rsbuf[0][:],
                                   start=True, stop=True)
                k.inc(mm, "t", ("bc", 0))
                for t in range(NT):
                    b, i = t // TPB, t % TPB
                    if t + 1 < NT:
                        k.wait(tensor, "v", ("xn", t, 7))
                        k.wait(tensor, "dst0", ("stm", t + 1))
                        k.wait(tensor, "dst1", ("str", t + 1))
                        tensor.matmul(p_mr[:], lhsT=sonesr[:], rhs=mrbuf[(t + 1) % 2][:],
                                      start=True, stop=True)
                        mm = tensor.matmul(p_rs[:], lhsT=sonesr[:], rhs=rsbuf[(t + 1) % 2][:],
                                           start=True, stop=True)
                        k.inc(mm, "t", ("bc", t + 1))
                    for f in range(NF):
                        if f == 0:
                            k.wait(tensor, "v", ("xn", t, 7))
                            if t > 0:
                                k.wait(tensor, "s", ("silu", t - 1, 6))
                        elif f == 1:
                            if t > 0:
                                k.wait(tensor, "s", ("silu", t - 1, 7))
                        else:
                            k.wait(tensor, "s", ("gelu", t, f - 2))
                        for e in range(NE):
                            mm = tensor.matmul(p_mm[f % 2][:],
                                               lhsT=w1t8[:, e * FF + f * 128: e * FF + (f + 1) * 128],
                                               rhs=xnt[t % 2][:, e * T:(e + 1) * T],
                                               start=(e == 0), stop=(e == NE - 1))
                        k.inc(mm, "t", ("mm1", t, f))
                    if t > 0:
                        k.wait(tensor, "v", ("a", t - 1, 6))
                    k.inc(tensor.matmul(p_lie[0][0:64, :], lhsT=Gb[:, b * 64:(b + 1) * 64],
                                        rhs=xnt[t % 2][0:64, 0:T], start=True, stop=True),
                          "t", ("xr", t))
                    for e in range(NE):
                        if e == 0:
                            k.wait(tensor, "s", ("xrev", t))
                        elif e == 1:
                            if t > 0:
                                k.wait(tensor, "v", ("a", t - 1, 7))
                        else:
                            k.wait(tensor, "v", ("a", t, e - 2))
                        k.inc(tensor.matmul(p_lie[e % 2][:], lhsT=woos[:, e * 128:(e + 1) * 128],
                                            rhs=xr_sb[:], start=True, stop=True),
                              "t", ("lie", t, e))

                    def mm2_group(e2):
                        if e2 == 0:
                            k.wait(tensor, "s", ("gelu", t, NF - 1))
                        elif e2 >= 2:
                            k.wait(tensor, "s", ("silu", t, e2 - 2))
                        for f in range(NF):
                            mm = tensor.matmul(p_mm[e2 % 2][:],
                                               lhsT=w2t8[:, f * EMB + e2 * 128: f * EMB + (e2 + 1) * 128],
                                               rhs=hT[:, f * T:(f + 1) * T],
                                               start=(f == 0), stop=(f == NF - 1))
                        k.inc(mm, "t", ("mm2", t, e2))
                    mm2_group(0)
                    mm2_group(1)
                    for e in range(NE):
                        k.wait(tensor, "s", ("sq", t, e))
                        if e == 0 and t > 0:
                            k.wait(tensor, "s", ("sqrt", t - 1))
                        k.inc(tensor.matmul(p_ms[:], lhsT=sonescb[:], rhs=sqb[e][:],
                                            start=(e == 0), stop=(e == NE - 1)),
                              "t", ("ms", t, e))
                    mm2_group(2)
                    k.wait(tensor, "v", ("recip", t))
                    if t > 0:
                        k.wait(tensor, "v", ("sg", t - 1, 7))
                    k.inc(tensor.matmul(p_rstd[:], lhsT=sonesr[:], rhs=rstd_row[:],
                                        start=True, stop=True), "t", ("rstdbc", t))
                    for e2 in range(3, NE):
                        mm2_group(e2)
            k.body(blk, "tensor", tensor_body)

            def vector_body(vector):
                for e in range(NE):
                    k.wait(vector, f"dxl{e % 2}", ("xln", 0, e))
                    k.wait(vector, "t", ("bc", 0))
                    vector.tensor_tensor(ubuf[e % 2][:], xlnb[e % 2][:], p_rs[:], op=ALU.mult)
                    k.inc(vector.tensor_tensor(xnt[0][:, e * T:(e + 1) * T], ubuf[e % 2][:],
                                               p_mr[:], op=ALU.subtract), "v", ("xn", 0, e))
                for t in range(NT):
                    b = t // TPB
                    if t + 1 < NT:
                        for e in range(NE):
                            k.wait(vector, f"dxl{e % 2}", ("xln", t + 1, e))
                            k.wait(vector, "t", ("bc", t + 1))
                            if t >= 1:
                                k.wait(vector, "t", ("xr", t - 1))
                            vector.tensor_tensor(ubuf[e % 2][:], xlnb[e % 2][:], p_rs[:],
                                                 op=ALU.mult)
                            k.inc(vector.tensor_tensor(xnt[(t + 1) % 2][:, e * T:(e + 1) * T],
                                                       ubuf[e % 2][:], p_mr[:], op=ALU.subtract),
                                  "v", ("xn", t + 1, e))
                    for e in range(NE):
                        k.wait(vector, "t", ("lie", t, e))
                        if e >= 2:
                            k.wait(vector, "s", ("sq", t, e - 2))
                        k.inc(vector.scalar_tensor_tensor(
                            out=abuf[e % 2][:], in0=p_lie[e % 2][:],
                            scalar=constPP[:, e * 2 + b:e * 2 + b + 1],
                            in1=xnt[t % 2][:, e * T:(e + 1) * T],
                            op0=ALU.add, op1=ALU.add), "v", ("a", t, e))
                        vector.tensor_scalar(attn[:, e * T:(e + 1) * T], abuf[e % 2][:],
                                             scalar1=snw[:, e:e + 1], scalar2=None, op0=ALU.mult)
                    k.wait(vector, "s", ("sqrt", t))
                    if t >= 1:
                        k.wait(vector, "t", ("rstdbc", t - 1))
                    k.inc(vector.reciprocal(rstd_row[:], rms_row[:]), "v", ("recip", t))
                    for e2 in range(NE):
                        k.wait(vector, "s", ("silu", t, e2))
                        k.wait(vector, "t", ("rstdbc", t))
                        k.wait(vector, f"dxr{e2 % 2}", ("xres", t, e2))
                        if (t, e2) > (0, 1):
                            pe = e2 - 2
                            pt = t
                            if pe < 0:
                                pe += 8
                                pt -= 1
                            k.wait(vector, f"do{pe % 2}", ("od", pt, pe))
                        vector.tensor_tensor(outb[e2 % 2][:], gtb[e2 % 2][:], p_rstd[:],
                                             op=ALU.mult)
                        vector.tensor_tensor(outb[e2 % 2][:], outb[e2 % 2][:],
                                             attn[:, e2 * T:(e2 + 1) * T], op=ALU.mult)
                        k.inc(vector.tensor_tensor(outb[e2 % 2][:], outb[e2 % 2][:],
                                                   xresb[e2 % 2][:], op=ALU.add),
                              "v", ("sg", t, e2))
            k.body(blk, "vector", vector_body)

            def scalar_body(scalar):
                for t in range(NT):
                    for f in range(NF):
                        k.wait(scalar, "t", ("mm1", t, f))
                        k.inc(scalar.activation(hT[:, f * T:(f + 1) * T], p_mm[f % 2][:],
                                                AF.Gelu, bias=sb1[:, f:f + 1]),
                              "s", ("gelu", t, f))
                    k.wait(scalar, "t", ("xr", t))
                    if t > 0:
                        k.wait(scalar, "t", ("lie", t - 1, 7))
                    k.inc(scalar.copy(xr_sb[:], p_lie[0][0:64, :]), "s", ("xrev", t))
                    for e in range(NE):
                        k.wait(scalar, "v", ("a", t, e))
                        if t > 0:
                            k.wait(scalar, "t", ("ms", t - 1, e))
                        k.inc(scalar.square(sqb[e][:], abuf[e % 2][:]), "s", ("sq", t, e))
                    k.wait(scalar, "t", ("ms", t, NE - 1))
                    if t > 0:
                        k.wait(scalar, "v", ("recip", t - 1))
                    k.inc(scalar.activation(rms_row[:], p_ms[:], AF.Sqrt,
                                            bias=scpool[0:1, 0:1], scale=1.0 / EMB),
                          "s", ("sqrt", t))
                    for e2 in range(NE):
                        k.wait(scalar, "t", ("mm2", t, e2))
                        if (t, e2) > (0, 1):
                            pe = e2 - 2
                            pt = t
                            if pe < 0:
                                pe += 8
                                pt -= 1
                            k.wait(scalar, "v", ("sg", pt, pe))
                        k.inc(scalar.activation(gtb[e2 % 2][:], p_mm[e2 % 2][:], AF.Silu,
                                                bias=sb2[:, e2:e2 + 1]),
                              "s", ("silu", t, e2))
            k.body(blk, "scalar", scalar_body)


def build_program():
    nc = bass.Bass("TRN2", target_bir_lowering=False, debug=False)
    io = {}
    io["xTd"] = nc.dram_tensor("xTd", [NB * EMB, SEQ], F32, kind="ExternalInput").ap()
    io["w1td"] = nc.dram_tensor("w1td", [EMB, FF], F32, kind="ExternalInput").ap()
    io["w2td"] = nc.dram_tensor("w2td", [FF, EMB], F32, kind="ExternalInput").ap()
    io["wtriud"] = nc.dram_tensor("wtriud", [EMB, FF], F32, kind="ExternalInput").ap()
    io["btriud"] = nc.dram_tensor("btriud", [64, 64], F32, kind="ExternalInput").ap()
    io["wooTd"] = nc.dram_tensor("wooTd", [64, EMB], F32, kind="ExternalInput").ap()
    io["wviTd"] = nc.dram_tensor("wviTd", [128, 512], F32, kind="ExternalInput").ap()
    io["wvoTd"] = nc.dram_tensor("wvoTd", [64, EMB], F32, kind="ExternalInput").ap()
    io["cbd"] = nc.dram_tensor("cbd", [128, 128], F32, kind="ExternalInput").ap()
    io["b1d"] = nc.dram_tensor("b1d", [128, 32], F32, kind="ExternalInput").ap()
    io["b2d"] = nc.dram_tensor("b2d", [128, 8], F32, kind="ExternalInput").ap()
    io["nwd"] = nc.dram_tensor("nwd", [128, 8], F32, kind="ExternalInput").ap()
    io["bvi2d"] = nc.dram_tensor("bvi2d", [2, 64], F32, kind="ExternalInput").ap()
    io["bvo2d"] = nc.dram_tensor("bvo2d", [2, EMB], F32, kind="ExternalInput").ap()
    io["boo2d"] = nc.dram_tensor("boo2d", [2, EMB], F32, kind="ExternalInput").ap()
    io["mixd"] = nc.dram_tensor("mixd", [128, 2], F32, kind="ExternalInput").ap()
    io["identd"] = nc.dram_tensor("identd", [128, 128], F32, kind="ExternalInput").ap()
    io["onesrd"] = nc.dram_tensor("onesrd", [1, 128], F32, kind="ExternalInput").ap()
    io["onescbd"] = nc.dram_tensor("onescbd", [128, 1], BF16, kind="ExternalInput").ap()
    io["onescfd"] = nc.dram_tensor("onescfd", [128, 1], F32, kind="ExternalInput").ap()
    io["cpoold"] = nc.dram_tensor("cpoold", [128, 4], F32, kind="ExternalInput").ap()
    io["omdram"] = nc.dram_tensor("omdram", [NB, 4096], F32, kind="Internal").ap()
    io["stats_d"] = nc.dram_tensor("stats_d", [4, SEQ], F32, kind="Internal").ap()
    io["outTd"] = nc.dram_tensor("outTd", [NB * EMB, SEQ], F32, kind="ExternalOutput").ap()
    if DEBUG_DUMP:
        _dtmap = {"FP8": FP8, "F32": F32, "BF16": BF16}
        for (nm, shape, dt) in DBG_SPECS:
            io["dbg_" + nm] = nc.dram_tensor("dbg_" + nm, shape, _dtmap[dt],
                                             kind="ExternalOutput").ap()
        for (nm, shape) in DBG_TW:
            io["dbg_" + nm] = nc.dram_tensor("dbg_" + nm, shape, F32,
                                             kind="ExternalOutput").ap()

    k1 = K(False)
    construct(k1, {n: Fake() for n in io})
    k2 = K(True, nc, M=k1.M)
    with k2.es:
        construct(k2, io)
    return nc


def kernel(L0_attn_x, L0_attn_omega_proj_w, L0_attn_omega_proj_b, L0_attn_omega_out_w,
           L0_attn_omega_out_b, L0_attn_vq_proj_in_w, L0_attn_vq_proj_in_b,
           L0_attn_vq_proj_out_w, L0_attn_vq_proj_out_b, L0_attn_codebook,
           L0_attn_omega_mix, L0_attn_vq_mix, L0_attn_norm_w, L0_ffn_W1, L0_ffn_b1,
           L0_ffn_W2, L0_ffn_b2):
    f32 = np.float32
    x = np.asarray(L0_attn_x, f32)
    xT = np.ascontiguousarray(x.transpose(0, 2, 1))
    w1t = np.ascontiguousarray(np.asarray(L0_ffn_W1, f32).T)
    w2t = np.ascontiguousarray(np.asarray(L0_ffn_W2, f32).T)

    opw = np.asarray(L0_attn_omega_proj_w, f32)
    opb = np.asarray(L0_attn_omega_proj_b, f32)
    R, C = np.triu_indices(64, k=1)
    wtriu = np.zeros((EMB, 64, 64), f32)
    wtriu[:, R, C] = opw.T
    wtriu = np.ascontiguousarray(wtriu.reshape(EMB, FF))
    btriu = np.zeros((64, 64), f32)
    btriu[R, C] = opb

    wooT = np.ascontiguousarray(np.asarray(L0_attn_omega_out_w, f32).T)
    wviT = np.asarray(L0_attn_vq_proj_in_w, f32).T
    wviT = np.ascontiguousarray(wviT.reshape(8, 128, 64).transpose(1, 0, 2).reshape(128, 512))
    wvoT = np.ascontiguousarray(np.asarray(L0_attn_vq_proj_out_w, f32).T)
    cb = np.asarray(L0_attn_codebook, f32)
    cbPP = np.ascontiguousarray(cb.reshape(2, 128, 64).transpose(1, 0, 2).reshape(128, 128))
    b1PP = np.ascontiguousarray(np.asarray(L0_ffn_b1, f32).reshape(32, 128).T)
    b2PP = np.ascontiguousarray(np.asarray(L0_ffn_b2, f32).reshape(8, 128).T)
    nwPP = np.ascontiguousarray(np.asarray(L0_attn_norm_w, f32).reshape(8, 128).T)
    bvi2 = np.ascontiguousarray(np.broadcast_to(np.asarray(L0_attn_vq_proj_in_b, f32), (2, 64)))
    bvo2 = np.ascontiguousarray(np.broadcast_to(np.asarray(L0_attn_vq_proj_out_b, f32), (2, EMB)))
    boo2 = np.ascontiguousarray(np.broadcast_to(np.asarray(L0_attn_omega_out_b, f32), (2, EMB)))
    mixd = np.empty((128, 2), f32)
    mixd[:, 0] = np.float32(L0_attn_omega_mix)
    mixd[:, 1] = np.float32(L0_attn_vq_mix)
    identd = np.eye(128, dtype=f32)
    onesrd = np.ones((1, 128), f32)
    onescbd = np.ones((128, 1), ml_dtypes.bfloat16)
    onescfd = np.ones((128, 1), f32)
    cpoold = np.broadcast_to(np.array([1e-6, 1e-5, 0.0, 0.0], f32), (128, 4)).copy()

    shared = {
        "w1td": w1t, "w2td": w2t, "wtriud": wtriu, "btriud": btriu, "wooTd": wooT,
        "wviTd": wviT, "wvoTd": wvoT, "cbd": cbPP, "b1d": b1PP, "b2d": b2PP,
        "nwd": nwPP, "bvi2d": bvi2, "bvo2d": bvo2, "boo2d": boo2, "mixd": mixd,
        "identd": identd, "onesrd": onesrd, "onescbd": onescbd, "onescfd": onescfd,
        "cpoold": cpoold,
    }
    in_maps = []
    for c in range(NCORES):
        m = dict(shared)
        m["xTd"] = np.ascontiguousarray(xT[2 * c:2 * c + 2].reshape(NB * EMB, SEQ))
        in_maps.append(m)

    nc = build_program()
    res = run_bass_kernel_spmd(nc, in_maps, core_ids=list(range(NCORES)))
    outs = []
    for c in range(NCORES):
        o = np.asarray(res.results[c]["outTd"], f32).reshape(NB, EMB, SEQ)
        outs.append(o.transpose(0, 2, 1))
    return np.ascontiguousarray(np.concatenate(outs, axis=0))
